# version 04
# baseline (speedup 1.0000x reference)
import numpy as np
import concourse.bass as bass
import concourse.mybir as mybir
from concourse.bass_utils import run_bass_kernel_spmd

F32 = mybir.dt.float32
BF16 = mybir.dt.bfloat16
AF = mybir.ActivationFunctionType
ALU = mybir.AluOpType
AXX = mybir.AxisListType.X

NCORES = 8
SEQ = 2048
DM = 1024
TILE_T = 1024
Q = 128
DIN = 2048
CONV_DIM = 2176
DPROJ = 4256
MLPI = 4096
DHID = 4096
EPS = 1e-5

ENGINES = ("sync", "tensor", "vector", "scalar", "gpsimd")


class Sem:
    __slots__ = ("h", "n", "name")

    def __init__(self, h, name):
        self.h = h
        self.n = 0
        self.name = name


class Plan:
    def __init__(self):
        self.prog = {e: [] for e in ENGINES}

    def do(self, eng, fn):
        self.prog[eng].append(fn)

    def wait(self, eng, sem, val=None):
        v = sem.n if val is None else val
        assert v <= sem.n, f"wait {sem.name}>={v} but planned only {sem.n}"
        if v > 0:
            self.do(eng, lambda e, _s=sem.h, _v=v: e.wait_ge(_s, _v))

    def op(self, eng, fn, inc=None, dma=False):
        if inc is not None:
            k = 16 if dma else 1
            inc.n += k
            self.do(eng, lambda e, _f=fn, _s=inc.h, _k=k: _f(e).then_inc(_s, _k))
            return inc.n
        self.do(eng, fn)
        return None

    def drain(self, eng):
        self.do(eng, lambda e: e.drain())

    def run(self, eng_name, eng):
        for fn in self.prog[eng_name]:
            fn(eng)


def _build(debug=False, upto=99, ntiles=4):
    import ml_dtypes
    np_bf16 = ml_dtypes.bfloat16
    nc = bass.Bass("TRN2", target_bir_lowering=False, debug=False)

    def din(name, shape, dt=F32):
        return nc.dram_tensor(name, list(shape), dt, kind="ExternalInput").ap()

    x_in = din("x", (2, SEQ, DM))
    w_in_b = din("w_in_b", (33, 128, 1024))
    w_dt_b = din("w_dt_b", (1, 128, 256))
    w_out_b = din("w_out_b", (8, 128, 2048))
    mlp1_b = din("mlp1_b", (32, 128, 1024))
    mlp2_b = din("mlp2_b", (8, 128, 4096))
    fw1_b = din("fw1_b", (32, 128, 1024))
    fw2_b = din("fw2_b", (32, 128, 1024))
    fh1_b = din("fh1_b", (32, 128, 4096))
    fh2_b = din("fh2_b", (32, 128, 4096))
    fw3_b = din("fw3_b", (8, 128, 4096))
    conv_w = din("conv_w", (17, 128, 4))
    conv_b = din("conv_b", (17, 128))
    dt_bias = din("dt_bias", (32, 1))
    a_log = din("A_log", (32, 1))
    d_param = din("D_param", (128, 32))
    norm_w = din("norm_w", (16, 128))
    rms_w = din("rms_w", (8, 128))
    mlp_b1 = din("mlp_b1", (32, 128))
    mlp_b2 = din("mlp_b2", (8, 128))
    f_s1 = din("f_s1", (32, 128))
    f_s2 = din("f_s2", (32, 128))
    f_sh1 = din("f_sh1", (32, 128))
    f_sh2 = din("f_sh2", (32, 128))
    f_s3 = din("f_s3", (8, 128))
    out_t = nc.dram_tensor("out", [2, SEQ, DM], F32, kind="ExternalOutput").ap()

    ident16_h = nc.inline_tensor(np.eye(128, dtype=np_bf16), "identc16").ap()
    identf_h = nc.inline_tensor(np.eye(128, dtype=np.float32), "identcf").ap()
    u01_h = nc.inline_tensor(np.triu(np.ones((128, 128))).astype(np_bf16), "u01c").ap()
    i32f_h = nc.inline_tensor(np.eye(32, dtype=np.float32), "i32fc").ap()
    i32b_h = nc.inline_tensor(np.eye(32, dtype=np_bf16), "i32bc").ap()
    ones1_h = nc.inline_tensor(np.ones((1, 128), dtype=np.float32), "ones1c").ap()
    ones16_h = nc.inline_tensor(np.ones((128, 128), dtype=np_bf16), "ones16c").ap()

    SB_BASE = 229376 - nc.sbuf_bytes_remaining
    SB_BASE = (SB_BASE + 63) // 64 * 64
    SB_TOP = 229376

    def at(name, off, shape, dt):
        assert off % 64 == 0, (name, off)
        nbytes = int(np.prod(shape[1:])) * (4 if dt == F32 else 2)
        assert SB_BASE + off + nbytes <= SB_TOP, (name, off, nbytes)
        return nc.alloc_sbuf_tensor_at(name, list(shape), dt,
                                       offset=SB_BASE + off)

    K = 1024
    x_tok = at("x_tok", 0, [128, 8192], F32)
    siluz16 = at("siluz16", 0, [128, 16384], BF16)
    hm16 = at("hm16", 0, [128, 8192], BF16)
    acc16 = at("acc16", 16 * K, [128, 8192], BF16)
    s1f = at("s1f", 0, [128, 1024], F32)
    xn16 = at("xn16", 32 * K, [128, 8192], BF16)
    cstage = at("cstage", 48 * K, [128, 2, 1032], F32)
    xbc16 = at("xbc16", 58 * K, [128, 17408], BF16)
    hn16 = at("hn16", 48 * K, [128, 8192], BF16)
    hscr = at("hscr", 64 * K, [128, 1024], F32)
    h16 = at("h16", 48 * K, [128, 32768], BF16)
    x_re = at("x_re", 48 * K, [128, 8192], F32)
    xn_tok16 = at("xn_tok16", 112 * K, [128, 8192], BF16)
    y16 = at("y16", 112 * K, [128, 16384], BF16)
    m116 = at("m116", 112 * K, [128, 32768], BF16)
    g116 = at("g116", 112 * K, [128, 32768], BF16)
    xd16 = at("xd16", 144 * K, [128, 2048], BF16)
    dm32 = at("dm32", 148 * K, [128, 128], F32)
    m16t = at("m16t", 148 * K + 512, [128, 128], BF16)
    sm16 = at("sm16", 148 * K + 768, [128, 128], BF16)
    g16 = at("g16", 149 * K, [128, 128], BF16)
    xds = at("xds", 149 * K + 256, [128, 2, 128], BF16)
    e32 = at("e32", 150 * K, [32, 1024], F32)
    dt32 = at("dt32", 154 * K, [32, 1024], F32)
    dtA32 = at("dtA32", 158 * K, [32, 1024], F32)
    dt16 = at("dt16", 162 * K, [32, 1024], BF16)
    cc16 = at("cc16", 164 * K, [64, 1024], BF16)
    c32 = at("c32", 166 * K, [32, 2, 128], F32)
    ctok32 = at("ctok32", 167 * K, [128, 2, 32], F32)
    clast32 = at("clast32", 167 * K + 256, [32, 2, 1], F32)
    clrow32 = at("clrow32", 167 * K + 320, [1, 2, 32], F32)
    wptok = at("wptok", 167 * K + 576, [128, 2, 32], F32)
    wpre = at("wpre", 167 * K + 832, [128, 2, 32], F32)
    eqb32 = at("eqb32", 168 * K + 64, [64, 2, 32], F32)
    expc16 = at("expc16", 168 * K + 320, [32, 2, 128], BF16)
    ce16 = at("ce16", 168 * K + 832, [64, 2, 128], BF16)
    btok16 = at("btok16", 169 * K + 320, [128, 2, 64], BF16)
    bt16 = at("bt16", 169 * K + 576, [128, 2, 64], BF16)
    hn16tmp = at("hn16tmp", 169 * K + 832, [64, 2, 64], BF16)
    sm16b = at("sm16b", 170 * K + 64, [128, 2, 128], BF16)
    scrf = at("scrf", 171 * K, [128, 1280], F32)
    hn32 = at("hn32", 176 * K, [64, 2048], F32)
    tails = at("tails", 184 * K, [128, 2, 17, 3], F32)
    sqa = at("sqa", 184 * K + 512, [128, 8], F32)
    msum = at("msum", 184 * K + 576, [128, 8], F32)
    mean_t = at("mean_t", 184 * K + 640, [128, 8], F32)
    m2t = at("m2t", 184 * K + 704, [128, 8], F32)
    vpre = at("vpre", 184 * K + 768, [128, 8], F32)
    rstd_t = at("rstd_t", 184 * K + 832, [128, 8], F32)
    expA = at("expA", 184 * K + 896, [32, 1], F32)
    cbias = at("cbias", 184 * K + 960, [128, 2], F32)
    ident16 = at("ident16", 185 * K, [128, 128], BF16)
    identf = at("identf", 185 * K + 256, [128, 128], F32)
    u01 = at("u01", 185 * K + 768, [128, 128], BF16)
    i32f = at("i32f", 186 * K, [32, 32], F32)
    i32b = at("i32b", 186 * K + 128, [32, 32], BF16)
    ones1 = at("ones1", 186 * K + 192, [1, 128], F32)
    ones16 = at("ones16", 186 * K + 704, [128, 128], BF16)
    cw_s = at("cw_s", 186 * K + 960, [128, 17, 4], F32)
    cb_s = at("cb_s", 187 * K + 256, [128, 17], F32)
    dtb_s = at("dtb_s", 187 * K + 384, [32, 1], F32)
    alog_s = at("alog_s", 187 * K + 448, [32, 1], F32)
    dp_s = at("dp_s", 187 * K + 512, [128, 32], F32)
    nw_s = at("nw_s", 187 * K + 640, [128, 16], F32)
    rw_s = at("rw_s", 187 * K + 704, [128, 8], F32)
    b1_s = at("b1_s", 187 * K + 768, [128, 32], F32)
    b2_s = at("b2_s", 187 * K + 896, [128, 8], F32)
    s1_s = at("s1_s", 187 * K + 960, [128, 32], F32)
    s2_s = at("s2_s", 188 * K + 64, [128, 32], F32)
    sh1_s = at("sh1_s", 188 * K + 192, [128, 32], F32)
    sh2_s = at("sh2_s", 188 * K + 320, [128, 32], F32)
    s3_s = at("s3_s", 188 * K + 448, [128, 8], F32)
    wbufs = [at(f"wbuf{i}", 189 * K + i * 4352, [128, 2176], BF16)
             for i in range(3)]

    p0 = nc.alloc_psum_tensor("pp0", [128, 512], F32)
    p1 = nc.alloc_psum_tensor("pp1", [128, 512], F32)
    psA = nc.alloc_psum_tensor("psA", [128, 512], F32)
    psB = nc.alloc_psum_tensor("psB", [128, 512], F32)
    psC = nc.alloc_psum_tensor("psC", [128, 512], F32)
    psD = nc.alloc_psum_tensor("psD", [128, 512], F32)
    psE = nc.alloc_psum_tensor("psE", [128, 512], F32)
    psT = nc.alloc_psum_tensor("psT", [128, 1024], BF16)

    dbg = {}
    if debug:
        for nm, shape, dt in [
            ("d_xn16", [128, 8192], BF16), ("d_siluz", [128, 16384], BF16),
            ("d_xbc", [128, 17408], BF16), ("d_dt", [32, 1024], F32),
            ("d_y", [128, 16384], BF16), ("d_yn", [128, 16384], BF16),
            ("d_hm", [128, 8192], BF16), ("d_m1", [128, 32768], BF16),
            ("d_h16", [128, 32768], BF16), ("d_g1", [128, 32768], BF16),
            ("d_acc", [128, 8192], BF16), ("d_stats", [128, 48], F32),
        ]:
            dbg[nm] = nc.dram_tensor(nm, shape, dt, kind="ExternalOutput").ap()

    sems = {}
    guards = []
    for nm in ("dW", "dG", "dX", "T", "V", "S", "G"):
        g = nc.semaphore(nm)
        sems[nm] = Sem(g.__enter__(), nm)
        guards.append(g)
    dW, dG, dX = sems["dW"], sems["dG"], sems["dX"]
    T, V, S = sems["T"], sems["V"], sems["S"]
    G = sems["G"]

    P = Plan()

    def mm(out_ap, lhsT_ap, rhs_ap, start, stop, inc=None):
        return P.op("tensor",
                    lambda e, o=out_ap, l=lhsT_ap, r=rhs_ap, st=start, sp=stop:
                    e.matmul(o, l, r, start=st, stop=sp), inc=inc)

    def transpose(out_ap, in_ap, ident_ap, inc=None):
        return P.op("tensor",
                    lambda e, o=out_ap, i=in_ap, d=ident_ap: e.transpose(o, i, d),
                    inc=inc)

    bank_free = {}

    def bank_wait(bank):
        if bank in bank_free:
            s, v = bank_free[bank]
            P.wait("tensor", s, v)

    def bank_consumed(bank, sem, val):
        bank_free[bank] = (sem, val)

    def barrier():
        for e in ENGINES:
            for s in (T, V, S, G, dW, dX, dG):
                P.wait(e, s)

    def load_consts():
        def ld(dst, src):
            P.op("sync", lambda e, d=dst, s=src: e.dma_start(
                out=d, in_=s, allow_slow_non_contiguous=True), inc=dX, dma=True)

        ld(ident16[:, :], ident16_h[:, :])
        ld(identf[:, :], identf_h[:, :])
        ld(u01[:, :], u01_h[:, :])
        ld(i32f[:, :], i32f_h[:, :])
        ld(i32b[:, :], i32b_h[:, :])
        ld(ones1[:, :], ones1_h[:, :])
        ld(ones16[:, :], ones16_h[:, :])
        ld(cw_s[:, :, :], conv_w.rearrange("c p k -> p c k"))
        ld(cb_s[:, :], conv_b.rearrange("c p -> p c"))
        ld(dtb_s[:, :], dt_bias[:, :])
        ld(alog_s[:, :], a_log[:, :])
        ld(dp_s[:, :], d_param[:, :])
        ld(nw_s[:, :], norm_w.rearrange("c p -> p c"))
        ld(rw_s[:, :], rms_w.rearrange("c p -> p c"))
        ld(b1_s[:, :], mlp_b1.rearrange("c p -> p c"))
        ld(b2_s[:, :], mlp_b2.rearrange("c p -> p c"))
        ld(s1_s[:, :], f_s1.rearrange("c p -> p c"))
        ld(s2_s[:, :], f_s2.rearrange("c p -> p c"))
        ld(sh1_s[:, :], f_sh1.rearrange("c p -> p c"))
        ld(sh2_s[:, :], f_sh2.rearrange("c p -> p c"))
        ld(s3_s[:, :], f_s3.rearrange("c p -> p c"))
        P.wait("scalar", dX)
        P.op("scalar", lambda e: e.activation(out=expA[:, :], in_=alog_s[:, :],
                                              func=AF.Exp), inc=S)
        P.wait("vector", dX)
        P.op("vector", lambda e: e.memset(tails[:, :, :, :], 0.0), inc=V)
        P.op("vector", lambda e: e.memset(cbias[:, 0:1], EPS), inc=V)
        P.op("vector", lambda e: e.memset(cbias[:, 1:2], 1.0), inc=V)

    wslot_state = {"i": 0}
    wslot_free = [None, None, None]

    def stream_weight_block(wblocks, m, mwidth, khalves, eng="gpsimd"):
        sem = dW if eng == "sync" else dG
        res = []
        for (k0, kn) in khalves:
            si = wslot_state["i"] % 3
            wslot_state["i"] += 1
            if wslot_free[si] is not None:
                s, v = wslot_free[si]
                P.wait(eng, s, v)
            src = wblocks[m, :, k0 * mwidth:(k0 + kn) * mwidth]
            dstap = wbufs[si][:, 0:kn * mwidth]
            tgt = P.op(eng,
                       lambda e, d=dstap, s=src: e.dma_start(out=d, in_=s),
                       inc=sem, dma=True)
            res.append((si, k0, kn, tgt, sem))
        return res

    def big_matmul(wblocks, KC, MC, rhs_fn, evac_fn, mwidths=None,
                   eng="gpsimd"):
        banks = [(p0, "p0"), (p1, "p1"), (psB, "psB"), (psC, "psC")]
        bi = 0
        khalves = [(0, KC)] if KC <= 17 else [(0, 16), (16, KC - 16)]
        for m in range(MC):
            mw = 128 if mwidths is None else mwidths[m]
            blocks = stream_weight_block(wblocks, m, mw, khalves, eng=eng)
            for nh in range(2):
                bank, bname = banks[bi % len(banks)]
                bi += 1
                bank_wait(bname)
                first = True
                tgt = None
                for (si, k0, kn, dtgt, dsem) in blocks:
                    P.wait("tensor", dsem, dtgt)
                    for k in range(kn):
                        lhs = wbufs[si][:, k * mw:(k + 1) * mw]
                        rhs = rhs_fn(k0 + k, nh)
                        last = (k0 + k == KC - 1)
                        tgt = mm(bank[0:mw, 0:512], lhs, rhs,
                                 start=first, stop=last,
                                 inc=T if last else None)
                        first = False
                ev = evac_fn(m, nh, bank[0:mw, 0:512], tgt)
                bank_consumed(bname, *ev)
            for (si, k0, kn, dtgt, dsem) in blocks:
                wslot_free[si] = (T, T.n)

    psT_guard = [None]

    def psT_wait(sl=None):
        if psT_guard[0] is not None:
            s, v = psT_guard[0]
            P.wait("tensor", s, v)

    def psT_done(sem, val):
        psT_guard[0] = (sem, val)

    def emit_xload(tile):
        seq, t0 = tile
        src = x_in[seq, t0:t0 + TILE_T, :].rearrange("(g p) f -> p g f", p=128)
        dst = x_tok[:, :].rearrange("p (g f) -> p g f", g=8)
        return P.op("sync", lambda e, d=dst, s=src: e.dma_start(out=d, in_=s),
                    inc=dX, dma=True)

    def emit_ln(tile, ldx):
        import os
        lnpart = int(os.environ.get("LNPART", "9"))
        seq, t0 = tile
        if lnpart < 2:
            return
        P.wait("scalar", dX, ldx)
        P.wait("vector", dX, ldx)
        P.op("vector", lambda e: e.memset(sqa[:, :], 0.0), inc=V)
        for g in range(8):
            gc = slice(g * 1024, (g + 1) * 1024)
            P.op("vector",
                 lambda e, gc=gc, g=g: e.reduce_sum(
                     out=msum[:, g:g + 1], in_=x_tok[:, gc], axis=AXX), inc=V)
        P.drain("vector")
        P.op("vector", lambda e: e.tensor_scalar(
            out=mean_t[:, :], in0=msum[:, :], scalar1=-1.0 / 1024, scalar2=None,
            op0=ALU.mult), inc=V)
        P.drain("vector")
        P.wait("scalar", V)
        for g in range(8):
            gc = slice(g * 1024, (g + 1) * 1024)
            P.op("scalar",
                 lambda e, gc=gc, g=g: e.activation(
                     out=scrf[:, 0:1024], in_=x_tok[:, gc], func=AF.Square,
                     bias=mean_t[:, g:g + 1],
                     accum_out=sqa[:, g:g + 1]), inc=S)
        P.op("scalar", lambda e: e.activation(
            out=rstd_t[:, :], in_=sqa[:, :], func=AF.Sqrt, scale=1.0 / 1024,
            bias=cbias[:, 0:1]), inc=S)
        P.wait("vector", S)
        P.op("vector", lambda e: e.reciprocal(
            out=rstd_t[:, :], in_=rstd_t[:, :]), inc=V)
        P.drain("vector")
        if lnpart < 3:
            return
        for g in range(8):
            gc = slice(g * 1024, (g + 1) * 1024)
            P.op("vector",
                 lambda e, gc=gc, g=g: e.tensor_scalar(
                     out=xn_tok16[:, gc], in0=x_tok[:, gc],
                     scalar1=mean_t[:, g:g + 1], scalar2=rstd_t[:, g:g + 1],
                     op0=ALU.add, op1=ALU.mult), inc=V)
        if dbg:
            for i, tt in enumerate((sqa, msum, mean_t, m2t, vpre, rstd_t)):
                P.op("vector", lambda e, i=i, tt=tt: e.tensor_copy(
                    out=scrf[:, 1280 + i * 8:1280 + i * 8 + 8], in_=tt[:, :]),
                    inc=V)
            vst = V.n
            P.wait("sync", V, vst)
            P.op("sync", lambda e: e.dma_start(
                out=dbg["d_stats"][:, 0:48], in_=scrf[:, 1280:1328]),
                inc=dX, dma=True)
        vdone = V.n
        if lnpart < 4:
            return
        P.wait("tensor", V, vdone)
        for g in range(8):
            for c in range(8):
                sl = (g * 8 + c) % 8
                psT_wait(sl)
                tt = transpose(psT[:, sl * 128:(sl + 1) * 128],
                               xn_tok16[:, g * 1024 + c * 128:
                                        g * 1024 + (c + 1) * 128],
                               ident16[:, :], inc=T)
                P.wait("scalar", T, tt)
                P.op("scalar",
                     lambda e, g=g, c=c, sl=sl: e.copy(
                         out=xn16[:, c * 1024 + g * 128:c * 1024 + g * 128 + 128],
                         in_=psT[:, sl * 128:(sl + 1) * 128]), inc=S)
                psT_done(S, S.n)

    def emit_win_conv(tile):
        seq, t0 = tile
        barrier()

        def rhs(k, nh):
            return xn16[:, k * 1024 + nh * 512:k * 1024 + (nh + 1) * 512]

        def evac(m, nh, psum_ap, mmtgt):
            if m < 16:
                P.wait("scalar", T, mmtgt)
                P.op("scalar",
                     lambda e, m=m, nh=nh, p=psum_ap: e.activation(
                         out=siluz16[:, m * 1024 + nh * 512:
                                     m * 1024 + (nh + 1) * 512],
                         in_=p, func=AF.Silu), inc=S)
                return (S, S.n)
            if m < 33:
                c = m - 16
                cs = cstage[:, c % 2, :]
                P.wait("vector", T, mmtgt)
                P.op("vector",
                     lambda e, cs=cs, nh=nh, p=psum_ap: e.tensor_copy(
                         out=cs[:, 3 + nh * 512:3 + (nh + 1) * 512], in_=p),
                     inc=V)
                if dbg and c < 16:
                    P.op("vector",
                         lambda e, c=c, nh=nh, p=psum_ap: e.tensor_copy(
                             out=y16[:, c * 1024 + nh * 512:
                                     c * 1024 + (nh + 1) * 512], in_=p), inc=V)
                ev = (V, V.n)
                if nh == 1:
                    P.drain("vector")
                    P.op("vector",
                         lambda e, cs=cs, c=c, seq=seq: e.tensor_copy(
                             out=cs[:, 0:3], in_=tails[:, seq, c, :]), inc=V)
                    P.op("vector",
                         lambda e, cs=cs, c=c, seq=seq: e.tensor_copy(
                             out=tails[:, seq, c, :], in_=cs[:, 1024:1027]),
                         inc=V)
                    P.drain("vector")
                    P.wait("vector", S)
                    P.op("vector",
                         lambda e, cs=cs, c=c: e.tensor_scalar(
                             out=scrf[:, 0:1024], in0=cs[:, 0:1024],
                             scalar1=cw_s[:, c, 0:1], scalar2=None,
                             op0=ALU.mult), inc=V)
                    for k in range(1, 4):
                        P.op("vector",
                             lambda e, cs=cs, c=c, k=k: e.scalar_tensor_tensor(
                                 out=scrf[:, 0:1024], in0=cs[:, k:k + 1024],
                                 scalar=cw_s[:, c, k:k + 1], in1=scrf[:, 0:1024],
                                 op0=ALU.mult, op1=ALU.add), inc=V)
                    P.wait("scalar", V)
                    P.op("scalar",
                         lambda e, c=c: e.activation(
                             out=xbc16[:, c * 1024:(c + 1) * 1024],
                             in_=scrf[:, 0:1024], func=AF.Silu,
                             bias=cb_s[:, c:c + 1]), inc=S)
                    ev = (S, S.n)
                return ev
            P.wait("scalar", T, mmtgt)
            P.op("scalar",
                 lambda e, nh=nh, p=psum_ap: e.activation(
                     out=e32[:, nh * 512:(nh + 1) * 512], in_=p,
                     func=AF.Exp, bias=dtb_s[:, :]), inc=S)
            return (S, S.n)

        big_matmul(w_in_b, 8, 33, rhs, evac)
        big_matmul(w_dt_b, 8, 1, lambda k, nh: rhs(k, nh),
                   lambda m, nh, pa, mt: evac(33, nh, pa, mt),
                   mwidths=[32])
        P.op("scalar", lambda e: e.activation(
            out=dt32[:, :], in_=e32[:, :], func=AF.Ln, bias=cbias[0:32, 1:2]), inc=S)
        sdt = S.n
        P.wait("vector", S, sdt)
        P.op("vector", lambda e: e.tensor_scalar(
            out=dtA32[:, :], in0=dt32[:, :], scalar1=expA[:, :], scalar2=-1.0,
            op0=ALU.mult, op1=ALU.mult), inc=V)
        P.op("vector", lambda e: e.tensor_copy(out=dt16[:, :], in_=dt32[:, :]),
             inc=V)
        P.op("vector", lambda e: e.tensor_copy(
            out=cc16[:, :], in_=xbc16[64:128, 16 * 1024:17 * 1024]), inc=V)

    def emit_ssd(tile):
        seq, t0 = tile
        if t0 == 0:
            P.op("vector", lambda e: e.memset(hn32[:, :], 0.0), inc=V)
        barrier()
        prev_dh = [None, None]
        parityT = [None, None]
        parityS = [None, None]
        parityG = [None, None]
        prev_hnup = None
        prev_ymm = None
        prev_gp = None
        for q in range(8):
            p = q % 2
            qc = slice(q * 128, (q + 1) * 128)
            if parityT[p] is not None:
                P.wait("vector", T, parityT[p])
                P.wait("vector", S, parityS[p])
                P.wait("vector", G, parityG[p])
                P.wait("scalar", T, parityT[p])
                P.wait("tensor", S, parityS[p])
                P.wait("tensor", G, parityG[p])
            P.op("vector", lambda e, qc=qc, p=p: e.tensor_tensor_scan(
                out=c32[:, p, :], data0=dtA32[:, qc], data1=dtA32[:, qc],
                initial=0.0, op0=ALU.add, op1=ALU.bypass), inc=V)
            P.op("vector", lambda e, qc=qc, p=p: e.reduce_sum(
                out=clast32[:, p, :], in_=dtA32[:, qc], axis=AXX), inc=V)
            vc = V.n
            bank_wait("psA")
            P.wait("tensor", V, vc)
            transpose(psA[0:128, 128:160], c32[:, p, :], identf[0:32, 0:32],
                      inc=T)
            tc = transpose(psA[0:1, 192:224], clast32[:, p, :],
                           identf[0:32, 0:32], inc=T)
            P.wait("vector", T, tc)
            P.op("vector", lambda e, p=p: e.tensor_copy(
                out=ctok32[:, p, :], in_=psA[0:128, 128:160]), inc=V)
            P.op("vector", lambda e, p=p: e.tensor_copy(
                out=clrow32[:, p, :], in_=psA[0:1, 192:224]), inc=V)
            P.drain("vector")
            vc2 = V.n
            P.wait("tensor", V, vc2)
            tclb = mm(psA[0:128, 224:256], ones1[:, :], clrow32[:, p, :],
                      start=True, stop=True, inc=T)
            P.wait("vector", T, tclb)
            P.op("vector", lambda e, p=p: e.tensor_tensor(
                out=wpre[:, p, :], in0=psA[0:128, 224:256],
                in1=ctok32[:, p, :], op=ALU.subtract), inc=V)
            P.wait("scalar", V)
            P.op("scalar", lambda e, p=p: e.activation(
                out=wptok[:, p, :], in_=wpre[:, p, :], func=AF.Exp), inc=S)
            P.op("scalar", lambda e, p=p: e.activation(
                out=eqb32[:, p, :], in_=psA[0:64, 224:256], func=AF.Exp),
                inc=S)
            P.op("scalar", lambda e, p=p: e.activation(
                out=expc16[:, p, :], in_=c32[:, p, :], func=AF.Exp), inc=S)
            shead = S.n
            bank_consumed("psA", S, shead)
            bank_wait("psB")
            smm = mm(psB[0:128, 128:256],
                     xbc16[0:64, 16 * 1024 + q * 128:16 * 1024 + (q + 1) * 128],
                     cc16[:, qc], start=True, stop=True, inc=T)
            P.wait("vector", T, smm)
            P.op("vector", lambda e, p=p: e.tensor_tensor(
                out=sm16b[:, p, :], in0=psB[0:128, 128:256], in1=u01[:, :],
                op=ALU.mult), inc=V)
            sm_done = V.n
            bank_consumed("psB", V, sm_done)
            psT_wait(4)
            tb = transpose(psT[:, 512:576],
                           xbc16[0:64, 16 * 1024 + q * 128:
                                 16 * 1024 + (q + 1) * 128],
                           ident16[0:64, 0:64], inc=T)
            P.wait("scalar", T, tb)
            P.op("scalar", lambda e, p=p: e.copy(out=btok16[:, p, :],
                                                 in_=psT[:, 512:576]), inc=S)
            btok_done = S.n
            psT_done(S, btok_done)
            pb = [(psB, "psB"), (psC, "psC")]
            for c in range(16):
                b, bn = pb[c % 2]
                bank_wait(bn)
                mm(b[0:64, 0:128],
                   i32b[:, 2 * c:2 * c + 1].broadcast_to([32, 64]),
                   dt16[:, qc], start=True, stop=True)
                m2 = mm(b[64:128, 0:128],
                        i32b[:, 2 * c + 1:2 * c + 2].broadcast_to([32, 64]),
                        dt16[:, qc], start=True, stop=True, inc=T)
                P.wait("vector", T, m2)
                P.op("vector",
                     lambda e, c=c, b=b, q=q: e.tensor_tensor(
                         out=xds[:, c % 2, :],
                         in0=xbc16[:, c * 1024 + q * 128:
                                   c * 1024 + (q + 1) * 128],
                         in1=b[0:128, 0:128], op=ALU.mult), inc=V)
                bank_consumed(bn, V, V.n)
                P.wait("tensor", V)
                sl = c % 4
                psT_wait(sl)
                tt = transpose(psT[:, sl * 128:(sl + 1) * 128],
                               xds[:, c % 2, :], ident16[:, :], inc=T)
                P.wait("scalar", T, tt)
                P.op("scalar",
                     lambda e, c=c, sl=sl: e.copy(
                         out=xd16[:, c * 128:(c + 1) * 128],
                         in_=psT[:, sl * 128:(sl + 1) * 128]), inc=S)
                psT_done(S, S.n)
            hb = [(psD, "psD"), (psE, "psE")]
            dbk = [(psB, "psB"), (psC, "psC")]
            if prev_hnup is not None:
                P.wait("gpsimd", V, prev_hnup)
            for h in range(32):
                X, Xn = hb[h % 2]
                DB, DBn = dbk[h % 2]
                hcol = slice(h * 64, (h + 1) * 64)
                xs_half = slice((h % 2) * 64, (h % 2) * 64 + 64)
                xchunk = h // 2
                bank_wait(DBn)
                bcd = mm(DB[0:128, 0:128],
                         i32f[:, h:h + 1].broadcast_to([32, 128]),
                         c32[:, p, :], start=True, stop=True, inc=T)
                P.wait("vector", T, bcd)
                P.op("vector",
                     lambda e, DB=DB, h=h, p=p: e.tensor_scalar(
                         out=dm32[:, :], in0=DB[0:128, 0:128],
                         scalar1=ctok32[:, p, h:h + 1], scalar2=0.0,
                         op0=ALU.subtract, op1=ALU.min), inc=V)
                bank_consumed(DBn, V, V.n)
                P.wait("scalar", V)
                if prev_gp is not None:
                    P.wait("scalar", G, prev_gp)
                P.op("scalar", lambda e: e.activation(
                    out=m16t[:, :], in_=dm32[:, :], func=AF.Exp), inc=S)
                P.wait("gpsimd", S)
                if prev_ymm is not None:
                    P.wait("gpsimd", T, prev_ymm)
                P.op("gpsimd", lambda e, p=p: e.tensor_tensor(
                    out=g16[:, :], in0=m16t[:, :], in1=sm16b[:, p, :],
                    op=ALU.mult), inc=G)
                g_done = G.n
                prev_gp = g_done
                P.op("gpsimd",
                     lambda e, h=h, hcol=hcol: e.tensor_copy(
                         out=hn16tmp[:, h % 2, :], in_=hn32[:, hcol]), inc=G)
                hcopy_done = G.n
                bank_wait(Xn)
                ce_mm = mm(X[0:64, 384:512],
                           i32b[:, h:h + 1].broadcast_to([32, 64]),
                           expc16[:, p, :], start=True, stop=True, inc=T)
                P.wait("vector", T, ce_mm)
                P.op("vector",
                     lambda e, X=X, qc=qc, h=h: e.tensor_tensor(
                         out=ce16[:, h % 2, :], in0=cc16[:, qc],
                         in1=X[0:64, 384:512], op=ALU.mult), inc=V)
                ce_done = V.n
                P.wait("scalar", S, btok_done)
                if prev_dh[h % 2] is not None:
                    P.wait("scalar", T, prev_dh[h % 2])
                P.op("scalar",
                     lambda e, h=h, p=p: e.activation(
                         out=bt16[:, h % 2, :], in_=btok16[:, p, :],
                         func=AF.Copy, scale=wptok[:, p, h:h + 1]), inc=S)
                bt_done = S.n
                P.wait("tensor", G, g_done)
                P.wait("tensor", V, ce_done)
                P.wait("tensor", G, hcopy_done)
                mm(X[0:64, 0:128], xd16[:, hcol], g16[:, :],
                   start=True, stop=False)
                ymm = mm(X[0:64, 0:128], hn16tmp[:, h % 2, :],
                         ce16[:, h % 2, :], start=False, stop=True, inc=T)
                prev_ymm = ymm
                P.wait("tensor", S, bt_done)
                dhmm = mm(X[0:64, 128:192], bt16[:, h % 2, :], xd16[:, hcol],
                          start=True, stop=True, inc=T)
                prev_dh[h % 2] = dhmm
                P.wait("vector", T, dhmm)
                P.op("vector",
                     lambda e, X=X, h=h, hcol=hcol, p=p: e.scalar_tensor_tensor(
                         out=hn32[:, hcol], in0=hn32[:, hcol],
                         scalar=eqb32[:, p, h:h + 1], in1=X[0:64, 128:192],
                         op0=ALU.mult, op1=ALU.add), inc=V)
                prev_hnup = V.n
                P.op("vector",
                     lambda e, X=X, h=h, q=q, xs_half=xs_half, xchunk=xchunk:
                     e.scalar_tensor_tensor(
                         out=y16[xs_half, xchunk * 1024 + q * 128:
                                 xchunk * 1024 + (q + 1) * 128],
                         in0=xbc16[xs_half, xchunk * 1024 + q * 128:
                                   xchunk * 1024 + (q + 1) * 128],
                         scalar=dp_s[xs_half, h:h + 1], in1=X[0:64, 0:128],
                         op0=ALU.mult, op1=ALU.add), inc=V)
                bank_consumed(Xn, V, V.n)
            parityT[p] = T.n
            parityS[p] = S.n
            parityG[p] = G.n

    def emit_gate_rms_wout(tile):
        barrier()
        for c in range(16):
            cc = slice(c * 1024, (c + 1) * 1024)
            P.op("vector",
                 lambda e, cc=cc: e.tensor_tensor(
                     out=y16[:, cc], in0=y16[:, cc], in1=siluz16[:, cc],
                     op=ALU.mult), inc=V)
        yg_done = V.n
        P.wait("scalar", V, yg_done)
        for c in range(16):
            cc = slice(c * 1024, (c + 1) * 1024)
            P.op("scalar",
                 lambda e, cc=cc: e.activation(
                     out=siluz16[:, cc], in_=y16[:, cc], func=AF.Square),
                 inc=S)
        sq_done = S.n
        for nh in range(2):
            bank, bname = [(p0, "p0"), (p1, "p1")][nh]
            bank_wait(bname)
            P.wait("tensor", S, sq_done)
            tgt = None
            for c in range(16):
                tgt = mm(bank[0:128, 0:512], ones16[:, :],
                         siluz16[:, c * 1024 + nh * 512:
                                 c * 1024 + (nh + 1) * 512],
                         start=(c == 0), stop=(c == 15),
                         inc=T if c == 15 else None)
            P.wait("scalar", T, tgt)
            P.op("scalar",
                 lambda e, bank=bank, nh=nh: e.activation(
                     out=scrf[:, nh * 512:(nh + 1) * 512],
                     in_=bank[0:128, 0:512], func=AF.Sqrt,
                     bias=cbias[:, 0:1], scale=1.0 / 2048), inc=S)
            bank_consumed(bname, S, S.n)
        rstd_done = S.n
        P.wait("vector", S, rstd_done)
        P.op("vector", lambda e: e.reciprocal(
            out=scrf[:, 0:1024], in_=scrf[:, 0:1024]), inc=V)
        for c in range(16):
            cc = slice(c * 1024, (c + 1) * 1024)
            P.op("vector",
                 lambda e, cc=cc, c=c: e.scalar_tensor_tensor(
                     out=y16[:, cc], in0=y16[:, cc], scalar=nw_s[:, c:c + 1],
                     in1=scrf[:, 0:1024], op0=ALU.mult, op1=ALU.mult), inc=V)
        yn_done = V.n
        P.wait("tensor", V, yn_done)

        def rhs(k, nh):
            return y16[:, k * 1024 + nh * 512:k * 1024 + (nh + 1) * 512]

        def evac(m, nh, psum_ap, mmtgt):
            P.wait("vector", T, mmtgt)
            sl = slice(m * 1024 + nh * 512, m * 1024 + (nh + 1) * 512)
            P.op("vector",
                 lambda e, sl=sl, p=psum_ap: e.tensor_tensor(
                     out=hm16[:, sl], in0=p, in1=xn16[:, sl], op=ALU.add),
                 inc=V)
            return (V, V.n)

        big_matmul(w_out_b, 16, 8, rhs, evac)

    def emit_mlp(tile):
        barrier()
        for c in range(8):
            cc = slice(c * 1024, (c + 1) * 1024)
            P.op("scalar",
                 lambda e, cc=cc: e.activation(
                     out=hn16[:, cc], in_=hm16[:, cc], func=AF.Square), inc=S)
        sq_done = S.n
        for nh in range(2):
            bank, bname = [(p0, "p0"), (p1, "p1")][nh]
            bank_wait(bname)
            P.wait("tensor", S, sq_done)
            tgt = None
            for c in range(8):
                tgt = mm(bank[0:128, 0:512], ones16[:, :],
                         hn16[:, c * 1024 + nh * 512:c * 1024 + (nh + 1) * 512],
                         start=(c == 0), stop=(c == 7),
                         inc=T if c == 7 else None)
            P.wait("scalar", T, tgt)
            P.op("scalar",
                 lambda e, bank=bank, nh=nh: e.activation(
                     out=hscr[:, nh * 512:(nh + 1) * 512],
                     in_=bank[0:128, 0:512], func=AF.Sqrt,
                     bias=cbias[:, 0:1], scale=1.0 / 1024), inc=S)
            bank_consumed(bname, S, S.n)
        rstd_done = S.n
        P.wait("vector", S, rstd_done)
        P.op("vector", lambda e: e.reciprocal(
            out=hscr[:, 0:1024], in_=hscr[:, 0:1024]), inc=V)
        for c in range(8):
            cc = slice(c * 1024, (c + 1) * 1024)
            P.op("vector",
                 lambda e, cc=cc, c=c: e.scalar_tensor_tensor(
                     out=hn16[:, cc], in0=hm16[:, cc], scalar=rw_s[:, c:c + 1],
                     in1=hscr[:, 0:1024], op0=ALU.mult, op1=ALU.mult), inc=V)
        hn_done = V.n
        P.wait("tensor", V, hn_done)

        def rhs1(k, nh):
            return hn16[:, k * 1024 + nh * 512:k * 1024 + (nh + 1) * 512]

        def evac1(m, nh, psum_ap, mmtgt):
            P.wait("scalar", T, mmtgt)
            P.op("scalar",
                 lambda e, m=m, nh=nh, p=psum_ap: e.activation(
                     out=m116[:, m * 1024 + nh * 512:m * 1024 + (nh + 1) * 512],
                     in_=p, func=AF.Silu, bias=b1_s[:, m:m + 1]), inc=S)
            return (S, S.n)

        big_matmul(mlp1_b, 8, 32, rhs1, evac1)
        barrier()

        def rhs2(k, nh):
            return m116[:, k * 1024 + nh * 512:k * 1024 + (nh + 1) * 512]

        def evac2(m, nh, psum_ap, mmtgt):
            P.wait("vector", T, mmtgt)
            sl = slice(m * 1024 + nh * 512, m * 1024 + (nh + 1) * 512)
            P.op("vector",
                 lambda e, sl=sl, p=psum_ap, m=m: e.scalar_tensor_tensor(
                     out=acc16[:, sl], in0=p, scalar=b2_s[:, m:m + 1],
                     in1=hm16[:, sl], op0=ALU.add, op1=ALU.add), inc=V)
            return (V, V.n)

        big_matmul(mlp2_b, 32, 8, rhs2, evac2)

    def emit_ffn(tile):
        barrier()
        banks = [(p0, "p0"), (p1, "p1"), (psB, "psB"), (psC, "psC")]
        bi = 0
        for m in range(32):
            bl1 = stream_weight_block(fw1_b, m, 128, [(0, 8)])
            bl2 = stream_weight_block(fw2_b, m, 128, [(0, 8)])
            for nh in range(2):
                ba, bna = banks[bi % 4]
                bb, bnb = banks[(bi + 1) % 4]
                bi += 2
                bank_wait(bna)
                t1 = None
                (si, k0, kn, dtgt, dsem) = bl1[0]
                P.wait("tensor", dsem, dtgt)
                for k in range(8):
                    t1 = mm(ba[0:128, 0:512], wbufs[si][:, k * 128:(k + 1) * 128],
                            xn16[:, k * 1024 + nh * 512:k * 1024 + (nh + 1) * 512],
                            start=(k == 0), stop=(k == 7),
                            inc=T if k == 7 else None)
                bank_wait(bnb)
                t2 = None
                (si2, k0, kn, dtgt2, dsem2) = bl2[0]
                P.wait("tensor", dsem2, dtgt2)
                for k in range(8):
                    t2 = mm(bb[0:128, 0:512], wbufs[si2][:, k * 128:(k + 1) * 128],
                            xn16[:, k * 1024 + nh * 512:k * 1024 + (nh + 1) * 512],
                            start=(k == 0), stop=(k == 7),
                            inc=T if k == 7 else None)
                P.wait("scalar", T, t1)
                P.wait("scalar", V)
                P.op("scalar",
                     lambda e, m=m, nh=nh, ba=ba: e.activation(
                         out=s1f[:, nh * 512:(nh + 1) * 512],
                         in_=ba[0:128, 0:512], func=AF.Silu,
                         scale=s1_s[:, m:m + 1]), inc=S)
                s1done = S.n
                bank_consumed(bna, S, s1done)
                P.wait("vector", T, t2)
                P.wait("vector", S, s1done)
                P.op("vector",
                     lambda e, m=m, nh=nh, bb=bb: e.scalar_tensor_tensor(
                         out=h16[:, m * 1024 + nh * 512:m * 1024 + (nh + 1) * 512],
                         in0=bb[0:128, 0:512], scalar=s2_s[:, m:m + 1],
                         in1=s1f[:, nh * 512:(nh + 1) * 512],
                         op0=ALU.mult, op1=ALU.mult), inc=V)
                bank_consumed(bnb, V, V.n)
            for (si, k0, kn, dtgt, dsem) in bl1 + bl2:
                wslot_free[si] = (T, T.n)

        barrier()

        def rhs_h(k, nh):
            return h16[:, k * 1024 + nh * 512:k * 1024 + (nh + 1) * 512]

        def evac_g1(m, nh, psum_ap, mmtgt):
            P.wait("scalar", T, mmtgt)
            P.op("scalar",
                 lambda e, m=m, nh=nh, p=psum_ap: e.activation(
                     out=g116[:, m * 1024 + nh * 512:m * 1024 + (nh + 1) * 512],
                     in_=p, func=AF.Gelu, scale=sh1_s[:, m:m + 1]), inc=S)
            return (S, S.n)

        big_matmul(fh1_b, 32, 32, rhs_h, evac_g1)
        barrier()

        def rhs_g1(k, nh):
            return g116[:, k * 1024 + nh * 512:k * 1024 + (nh + 1) * 512]

        def evac_g2(m, nh, psum_ap, mmtgt):
            P.wait("scalar", T, mmtgt)
            P.op("scalar",
                 lambda e, m=m, nh=nh, p=psum_ap: e.activation(
                     out=h16[:, m * 1024 + nh * 512:m * 1024 + (nh + 1) * 512],
                     in_=p, func=AF.Gelu, scale=sh2_s[:, m:m + 1]), inc=S)
            return (S, S.n)

        big_matmul(fh2_b, 32, 32, rhs_g1, evac_g2)
        barrier()

        def evac_w3(m, nh, psum_ap, mmtgt):
            P.wait("vector", T, mmtgt)
            sl = slice(m * 1024 + nh * 512, m * 1024 + (nh + 1) * 512)
            P.op("vector",
                 lambda e, sl=sl, p=psum_ap, m=m: e.scalar_tensor_tensor(
                     out=acc16[:, sl], in0=p, scalar=s3_s[:, m:m + 1],
                     in1=acc16[:, sl], op0=ALU.mult, op1=ALU.add), inc=V)
            return (V, V.n)

        big_matmul(fw3_b, 32, 8, rhs_h, evac_w3)

    def emit_final(tile):
        seq, t0 = tile
        barrier()
        src = x_in[seq, t0:t0 + TILE_T, :].rearrange("(g p) f -> p g f", p=128)
        dst = x_re[:, :].rearrange("p (g f) -> p g f", g=8)
        ldx = P.op("sync", lambda e, d=dst, s=src: e.dma_start(out=d, in_=s),
                   inc=dX, dma=True)
        P.wait("vector", dX, ldx)
        for c in range(8):
            for g in range(8):
                sl = (c * 8 + g) % 8
                psT_wait(sl)
                tt = transpose(psT[:, sl * 128:(sl + 1) * 128],
                               acc16[:, c * 1024 + g * 128:
                                     c * 1024 + g * 128 + 128],
                               ident16[:, :], inc=T)
                P.wait("vector", T, tt)
                P.op("vector",
                     lambda e, c=c, g=g, sl=sl: e.tensor_tensor(
                         out=x_re[:, g * 1024 + c * 128:g * 1024 + c * 128 + 128],
                         in0=x_re[:, g * 1024 + c * 128:g * 1024 + c * 128 + 128],
                         in1=psT[:, sl * 128:(sl + 1) * 128], op=ALU.add),
                     inc=V)
                psT_done(V, V.n)
        vout = V.n
        P.wait("sync", V, vout)
        osrc = x_re[:, :].rearrange("p (g f) -> p g f", g=8)
        odst = out_t[seq, t0:t0 + TILE_T, :].rearrange("(g p) f -> p g f", p=128)
        P.op("sync", lambda e, d=odst, s=osrc: e.dma_start(out=d, in_=s),
             inc=dX, dma=True)

    def dbg_dump(name, buf):
        if not dbg:
            return
        P.wait("sync", V)
        P.wait("sync", S)
        P.wait("sync", T)
        P.op("sync", lambda e, d=dbg[name], s=buf: e.dma_start(out=d, in_=s[:, :]),
             inc=dX, dma=True)

    load_consts()
    tiles = [(0, 0), (0, 1024), (1, 0), (1, 1024)][:ntiles]
    ldx0 = emit_xload(tiles[0]) if upto >= 1 else None
    for ti, tile in enumerate(tiles):
        if upto >= 1:
            ldx = ldx0 if ti == 0 else emit_xload(tile)
            emit_ln(tile, ldx)
            if debug and ti == 0:
                dbg_dump("d_xn16", xn16)
        if upto >= 2:
            emit_win_conv(tile)
            if debug and ti == 0:
                dbg_dump("d_siluz", siluz16)
                dbg_dump("d_xbc", xbc16)
                dbg_dump("d_dt", dt32)
                if upto == 2:
                    dbg_dump("d_y", y16)
        if upto >= 3:
            emit_ssd(tile)
            if debug and ti == 0:
                dbg_dump("d_y", y16)
        if upto >= 4:
            emit_gate_rms_wout(tile)
            if debug and ti == 0:
                dbg_dump("d_yn", y16)
                dbg_dump("d_hm", hm16)
        if upto >= 5:
            emit_mlp(tile)
            if debug and ti == 0:
                dbg_dump("d_m1", m116)
        if upto >= 6:
            emit_ffn(tile)
            if debug and ti == 0:
                dbg_dump("d_h16", h16)
                dbg_dump("d_g1", g116)
                dbg_dump("d_acc", acc16)
        if upto >= 7:
            emit_final(tile)
        barrier()
    P.wait("sync", dX)
    P.wait("sync", dW)
    P.wait("sync", dG)

    with nc.Block() as block:
        @block.sync
        def _(e):
            P.run("sync", e)

        @block.tensor
        def _(e):
            P.run("tensor", e)

        @block.vector
        def _(e):
            P.run("vector", e)

        @block.scalar
        def _(e):
            P.run("scalar", e)

        @block.gpsimd
        def _(e):
            P.run("gpsimd", e)

    for g in guards:
        g.__exit__(None, None, None)
    return nc


def _blockify(W, KC, mw=128):
    I, O = W.shape
    MC = O // mw
    return np.ascontiguousarray(
        W.reshape(KC, 128, MC, mw).transpose(2, 1, 0, 3).reshape(
            MC, 128, KC * mw).astype(np.float32))


_SHARED = {}


def _shared_inputs(inputs):
    f32 = np.float32
    g = lambda k: np.ascontiguousarray(np.asarray(inputs[k], dtype=f32))
    d = {}
    w_in = g("L0_attn_W_in")
    d["w_in_b"] = _blockify(w_in[:, :4224], 8)
    d["w_dt_b"] = _blockify(w_in[:, 4224:], 8, mw=32)
    d["w_out_b"] = _blockify(g("L0_attn_W_out"), 16)
    d["mlp1_b"] = _blockify(g("L0_attn_mlp_w1"), 8)
    d["mlp2_b"] = _blockify(g("L0_attn_mlp_w2"), 32)
    d["fw1_b"] = _blockify(g("L0_ffn_w1_w").T, 8)
    d["fw2_b"] = _blockify(g("L0_ffn_w2_w").T, 8)
    d["fh1_b"] = _blockify(g("L0_ffn_h1_w").T, 32)
    d["fh2_b"] = _blockify(g("L0_ffn_h2_w").T, 32)
    d["fw3_b"] = _blockify(g("L0_ffn_w3_w").T, 32)
    d["conv_w"] = np.ascontiguousarray(g("L0_attn_conv_w").reshape(17, 128, 4))
    d["conv_b"] = np.ascontiguousarray(g("L0_attn_conv_b").reshape(17, 128))
    d["dt_bias"] = np.ascontiguousarray(g("L0_attn_dt_bias").reshape(32, 1))
    d["A_log"] = np.ascontiguousarray(g("L0_attn_A_log").reshape(32, 1))
    d["D_param"] = np.ascontiguousarray(
        np.broadcast_to(g("L0_attn_D_param").reshape(1, 32), (128, 32)))
    d["norm_w"] = np.ascontiguousarray(g("L0_attn_norm_w").reshape(16, 128))
    d["rms_w"] = np.ascontiguousarray(g("L0_attn_rms_w").reshape(8, 128))
    d["mlp_b1"] = np.ascontiguousarray(g("L0_attn_mlp_b1").reshape(32, 128))
    d["mlp_b2"] = np.ascontiguousarray(g("L0_attn_mlp_b2").reshape(8, 128))
    d["f_s1"] = np.ascontiguousarray(g("L0_ffn_w1_s").reshape(32, 128))
    d["f_s2"] = np.ascontiguousarray(g("L0_ffn_w2_s").reshape(32, 128))
    d["f_sh1"] = np.ascontiguousarray(g("L0_ffn_h1_s").reshape(32, 128))
    d["f_sh2"] = np.ascontiguousarray(g("L0_ffn_h2_s").reshape(32, 128))
    d["f_s3"] = np.ascontiguousarray(g("L0_ffn_w3_s").reshape(8, 128))
    return d


def _host_inputs(inputs, core):
    if "d" not in _SHARED:
        _SHARED["d"] = _shared_inputs(inputs)
    d = dict(_SHARED["d"])
    x = np.ascontiguousarray(np.asarray(inputs["L0_attn_hidden_tokens"],
                                        dtype=np.float32))
    d["x"] = np.ascontiguousarray(x[2 * core:2 * core + 2])
    return d


def kernel(**inputs):
    _SHARED.clear()
    nc = _build(debug=False)
    in_maps = [_host_inputs(inputs, i) for i in range(NCORES)]
    res = run_bass_kernel_spmd(nc, in_maps, core_ids=list(range(NCORES)))
    out = np.concatenate([np.asarray(r["out"]) for r in res.results], axis=0)
    return np.ascontiguousarray(out.astype(np.float32))


# version 05
# speedup vs baseline: 1.0312x; 1.0312x over previous
import numpy as np
import concourse.bass as bass
import concourse.mybir as mybir
from concourse.bass_utils import run_bass_kernel_spmd

F32 = mybir.dt.float32
BF16 = mybir.dt.bfloat16
AF = mybir.ActivationFunctionType
ALU = mybir.AluOpType
AXX = mybir.AxisListType.X

NCORES = 8
SEQ = 2048
DM = 1024
TILE_T = 1024
Q = 128
DIN = 2048
CONV_DIM = 2176
DPROJ = 4256
MLPI = 4096
DHID = 4096
EPS = 1e-5

ENGINES = ("sync", "tensor", "vector", "scalar", "gpsimd")


class Sem:
    __slots__ = ("h", "n", "name")

    def __init__(self, h, name):
        self.h = h
        self.n = 0
        self.name = name


class Plan:
    def __init__(self):
        self.prog = {e: [] for e in ENGINES}

    def do(self, eng, fn):
        self.prog[eng].append(fn)

    def wait(self, eng, sem, val=None):
        v = sem.n if val is None else val
        assert v <= sem.n, f"wait {sem.name}>={v} but planned only {sem.n}"
        if v > 0:
            self.do(eng, lambda e, _s=sem.h, _v=v: e.wait_ge(_s, _v))

    def op(self, eng, fn, inc=None, dma=False):
        if inc is not None:
            k = 16 if dma else 1
            inc.n += k
            self.do(eng, lambda e, _f=fn, _s=inc.h, _k=k: _f(e).then_inc(_s, _k))
            return inc.n
        self.do(eng, fn)
        return None

    def drain(self, eng):
        self.do(eng, lambda e: e.drain())

    def run(self, eng_name, eng):
        for fn in self.prog[eng_name]:
            fn(eng)


def _build(debug=False, upto=99, ntiles=4):
    import ml_dtypes
    np_bf16 = ml_dtypes.bfloat16
    nc = bass.Bass("TRN2", target_bir_lowering=False, debug=False)

    def din(name, shape, dt=F32):
        return nc.dram_tensor(name, list(shape), dt, kind="ExternalInput").ap()

    x_in = din("x", (2, SEQ, DM))
    w_in_b = din("w_in_b", (33, 128, 1024))
    w_dt_b = din("w_dt_b", (1, 128, 256))
    w_out_b = din("w_out_b", (8, 128, 2048))
    mlp1_b = din("mlp1_b", (32, 128, 1024))
    mlp2_b = din("mlp2_b", (8, 128, 4096))
    fw1_b = din("fw1_b", (32, 128, 1024))
    fw2_b = din("fw2_b", (32, 128, 1024))
    fh1_b = din("fh1_b", (32, 128, 4096))
    fh2_b = din("fh2_b", (32, 128, 4096))
    fw3_b = din("fw3_b", (8, 128, 4096))
    conv_w = din("conv_w", (17, 128, 4))
    conv_b = din("conv_b", (17, 128))
    dt_bias = din("dt_bias", (32, 1))
    a_log = din("A_log", (32, 1))
    d_param = din("D_param", (128, 32))
    norm_w = din("norm_w", (16, 128))
    rms_w = din("rms_w", (8, 128))
    mlp_b1 = din("mlp_b1", (32, 128))
    mlp_b2 = din("mlp_b2", (8, 128))
    f_s1 = din("f_s1", (32, 128))
    f_s2 = din("f_s2", (32, 128))
    f_sh1 = din("f_sh1", (32, 128))
    f_sh2 = din("f_sh2", (32, 128))
    f_s3 = din("f_s3", (8, 128))
    out_t = nc.dram_tensor("out", [2, SEQ, DM], F32, kind="ExternalOutput").ap()

    ident16_h = nc.inline_tensor(np.eye(128, dtype=np_bf16), "identc16").ap()
    identf_h = nc.inline_tensor(np.eye(128, dtype=np.float32), "identcf").ap()
    u01_h = nc.inline_tensor(np.triu(np.ones((128, 128))).astype(np_bf16), "u01c").ap()
    i32f_h = nc.inline_tensor(np.eye(32, dtype=np.float32), "i32fc").ap()
    i32b_h = nc.inline_tensor(np.eye(32, dtype=np_bf16), "i32bc").ap()
    ones1_h = nc.inline_tensor(np.ones((1, 128), dtype=np.float32), "ones1c").ap()
    ones16_h = nc.inline_tensor(np.ones((128, 128), dtype=np_bf16), "ones16c").ap()

    SB_BASE = 229376 - nc.sbuf_bytes_remaining
    SB_BASE = (SB_BASE + 63) // 64 * 64
    SB_TOP = 229376

    def at(name, off, shape, dt):
        assert off % 64 == 0, (name, off)
        nbytes = int(np.prod(shape[1:])) * (4 if dt == F32 else 2)
        assert SB_BASE + off + nbytes <= SB_TOP, (name, off, nbytes)
        return nc.alloc_sbuf_tensor_at(name, list(shape), dt,
                                       offset=SB_BASE + off)

    K = 1024
    x_tok = at("x_tok", 0, [128, 8192], F32)
    siluz16 = at("siluz16", 0, [128, 16384], BF16)
    hm16 = at("hm16", 0, [128, 8192], BF16)
    acc16 = at("acc16", 16 * K, [128, 8192], BF16)
    s1f = at("s1f", 0, [128, 1024], F32)
    xn16 = at("xn16", 32 * K, [128, 8192], BF16)
    cstage = at("cstage", 48 * K, [128, 2, 1032], F32)
    xbc16 = at("xbc16", 58 * K, [128, 17408], BF16)
    hn16 = at("hn16", 48 * K, [128, 8192], BF16)
    hscr = at("hscr", 64 * K, [128, 1024], F32)
    h16 = at("h16", 48 * K, [128, 32768], BF16)
    x_re = at("x_re", 48 * K, [128, 8192], F32)
    xn_tok16 = at("xn_tok16", 112 * K, [128, 8192], BF16)
    y16 = at("y16", 112 * K, [128, 16384], BF16)
    m116 = at("m116", 112 * K, [128, 32768], BF16)
    g116 = at("g116", 112 * K, [128, 32768], BF16)
    xd16 = at("xd16", 144 * K, [128, 2048], BF16)
    dm32 = at("dm32", 148 * K, [128, 128], F32)
    m16t = at("m16t", 148 * K + 512, [128, 128], BF16)
    sm16 = at("sm16", 148 * K + 768, [128, 128], BF16)
    g16 = at("g16", 149 * K, [128, 128], BF16)
    xds = at("xds", 149 * K + 256, [128, 2, 128], BF16)
    e32 = at("e32", 150 * K, [32, 1024], F32)
    dt32 = at("dt32", 154 * K, [32, 1024], F32)
    dtA32 = at("dtA32", 158 * K, [32, 1024], F32)
    dt16 = at("dt16", 162 * K, [32, 1024], BF16)
    cc16 = at("cc16", 164 * K, [64, 1024], BF16)
    c32 = at("c32", 166 * K, [32, 2, 128], F32)
    ctok32 = at("ctok32", 167 * K, [128, 2, 32], F32)
    clast32 = at("clast32", 167 * K + 256, [32, 2, 1], F32)
    clrow32 = at("clrow32", 167 * K + 320, [1, 2, 32], F32)
    wptok = at("wptok", 167 * K + 576, [128, 2, 32], F32)
    wpre = at("wpre", 167 * K + 832, [128, 2, 32], F32)
    eqb32 = at("eqb32", 168 * K + 64, [64, 2, 32], F32)
    expc16 = at("expc16", 168 * K + 320, [32, 2, 128], BF16)
    ce16 = at("ce16", 168 * K + 832, [64, 2, 128], BF16)
    btok16 = at("btok16", 169 * K + 320, [128, 2, 64], BF16)
    bt16 = at("bt16", 169 * K + 576, [128, 2, 64], BF16)
    hn16tmp = at("hn16tmp", 169 * K + 832, [64, 2, 64], BF16)
    sm16b = at("sm16b", 170 * K + 64, [128, 2, 128], BF16)
    scrf = at("scrf", 171 * K, [128, 1280], F32)
    hn32 = at("hn32", 176 * K, [64, 2048], F32)
    tails = at("tails", 184 * K, [128, 2, 17, 3], F32)
    sqa = at("sqa", 184 * K + 512, [128, 8], F32)
    msum = at("msum", 184 * K + 576, [128, 8], F32)
    mean_t = at("mean_t", 184 * K + 640, [128, 8], F32)
    m2t = at("m2t", 184 * K + 704, [128, 8], F32)
    vpre = at("vpre", 184 * K + 768, [128, 8], F32)
    rstd_t = at("rstd_t", 184 * K + 832, [128, 8], F32)
    expA = at("expA", 184 * K + 896, [32, 1], F32)
    cbias = at("cbias", 184 * K + 960, [128, 2], F32)
    ident16 = at("ident16", 185 * K, [128, 128], BF16)
    identf = at("identf", 185 * K + 256, [128, 128], F32)
    u01 = at("u01", 185 * K + 768, [128, 128], BF16)
    i32f = at("i32f", 186 * K, [32, 32], F32)
    i32b = at("i32b", 186 * K + 128, [32, 32], BF16)
    ones1 = at("ones1", 186 * K + 192, [1, 128], F32)
    ones16 = at("ones16", 186 * K + 704, [128, 128], BF16)
    cw_s = at("cw_s", 186 * K + 960, [128, 17, 4], F32)
    cb_s = at("cb_s", 187 * K + 256, [128, 17], F32)
    dtb_s = at("dtb_s", 187 * K + 384, [32, 1], F32)
    alog_s = at("alog_s", 187 * K + 448, [32, 1], F32)
    dp_s = at("dp_s", 187 * K + 512, [128, 32], F32)
    nw_s = at("nw_s", 187 * K + 640, [128, 16], F32)
    rw_s = at("rw_s", 187 * K + 704, [128, 8], F32)
    b1_s = at("b1_s", 187 * K + 768, [128, 32], F32)
    b2_s = at("b2_s", 187 * K + 896, [128, 8], F32)
    s1_s = at("s1_s", 187 * K + 960, [128, 32], F32)
    s2_s = at("s2_s", 188 * K + 64, [128, 32], F32)
    sh1_s = at("sh1_s", 188 * K + 192, [128, 32], F32)
    sh2_s = at("sh2_s", 188 * K + 320, [128, 32], F32)
    s3_s = at("s3_s", 188 * K + 448, [128, 8], F32)
    wbufs = [at(f"wbuf{i}", 189 * K + i * 4352, [128, 2176], BF16)
             for i in range(3)]

    p0 = nc.alloc_psum_tensor("pp0", [128, 512], F32)
    p1 = nc.alloc_psum_tensor("pp1", [128, 512], F32)
    psA = nc.alloc_psum_tensor("psA", [128, 512], F32)
    psB = nc.alloc_psum_tensor("psB", [128, 512], F32)
    psC = nc.alloc_psum_tensor("psC", [128, 512], F32)
    psD = nc.alloc_psum_tensor("psD", [128, 512], F32)
    psE = nc.alloc_psum_tensor("psE", [128, 512], F32)
    psT = nc.alloc_psum_tensor("psT", [128, 1024], BF16)

    dbg = {}
    if debug:
        for nm, shape, dt in [
            ("d_xn16", [128, 8192], BF16), ("d_siluz", [128, 16384], BF16),
            ("d_xbc", [128, 17408], BF16), ("d_dt", [32, 1024], F32),
            ("d_y", [128, 16384], BF16), ("d_yn", [128, 16384], BF16),
            ("d_hm", [128, 8192], BF16), ("d_m1", [128, 32768], BF16),
            ("d_h16", [128, 32768], BF16), ("d_g1", [128, 32768], BF16),
            ("d_acc", [128, 8192], BF16), ("d_stats", [128, 48], F32),
        ]:
            dbg[nm] = nc.dram_tensor(nm, shape, dt, kind="ExternalOutput").ap()

    sems = {}
    guards = []
    for nm in ("dW", "dG", "dX", "T", "V", "S", "G"):
        g = nc.semaphore(nm)
        sems[nm] = Sem(g.__enter__(), nm)
        guards.append(g)
    dW, dG, dX = sems["dW"], sems["dG"], sems["dX"]
    T, V, S = sems["T"], sems["V"], sems["S"]
    G = sems["G"]

    P = Plan()

    def mm(out_ap, lhsT_ap, rhs_ap, start, stop, inc=None):
        return P.op("tensor",
                    lambda e, o=out_ap, l=lhsT_ap, r=rhs_ap, st=start, sp=stop:
                    e.matmul(o, l, r, start=st, stop=sp), inc=inc)

    def transpose(out_ap, in_ap, ident_ap, inc=None):
        return P.op("tensor",
                    lambda e, o=out_ap, i=in_ap, d=ident_ap: e.transpose(o, i, d),
                    inc=inc)

    bank_free = {}

    def bank_wait(bank):
        if bank in bank_free:
            s, v = bank_free[bank]
            P.wait("tensor", s, v)

    def bank_consumed(bank, sem, val):
        bank_free[bank] = (sem, val)

    def barrier():
        for e in ENGINES:
            for s in (T, V, S, G, dW, dX, dG):
                P.wait(e, s)

    def load_consts():
        def ld(dst, src):
            P.op("sync", lambda e, d=dst, s=src: e.dma_start(
                out=d, in_=s, allow_slow_non_contiguous=True), inc=dX, dma=True)

        ld(ident16[:, :], ident16_h[:, :])
        ld(identf[:, :], identf_h[:, :])
        ld(u01[:, :], u01_h[:, :])
        ld(i32f[:, :], i32f_h[:, :])
        ld(i32b[:, :], i32b_h[:, :])
        ld(ones1[:, :], ones1_h[:, :])
        ld(ones16[:, :], ones16_h[:, :])
        ld(cw_s[:, :, :], conv_w.rearrange("c p k -> p c k"))
        ld(cb_s[:, :], conv_b.rearrange("c p -> p c"))
        ld(dtb_s[:, :], dt_bias[:, :])
        ld(alog_s[:, :], a_log[:, :])
        ld(dp_s[:, :], d_param[:, :])
        ld(nw_s[:, :], norm_w.rearrange("c p -> p c"))
        ld(rw_s[:, :], rms_w.rearrange("c p -> p c"))
        ld(b1_s[:, :], mlp_b1.rearrange("c p -> p c"))
        ld(b2_s[:, :], mlp_b2.rearrange("c p -> p c"))
        ld(s1_s[:, :], f_s1.rearrange("c p -> p c"))
        ld(s2_s[:, :], f_s2.rearrange("c p -> p c"))
        ld(sh1_s[:, :], f_sh1.rearrange("c p -> p c"))
        ld(sh2_s[:, :], f_sh2.rearrange("c p -> p c"))
        ld(s3_s[:, :], f_s3.rearrange("c p -> p c"))
        P.wait("scalar", dX)
        P.op("scalar", lambda e: e.activation(out=expA[:, :], in_=alog_s[:, :],
                                              func=AF.Exp), inc=S)
        P.wait("vector", dX)
        P.op("vector", lambda e: e.memset(tails[:, :, :, :], 0.0), inc=V)
        P.op("vector", lambda e: e.memset(cbias[:, 0:1], EPS), inc=V)
        P.op("vector", lambda e: e.memset(cbias[:, 1:2], 1.0), inc=V)

    wslot_state = {"i": 0}
    wslot_free = [None, None, None]

    def stream_weight_block(wblocks, m, mwidth, khalves, eng="gpsimd"):
        sem = dW if eng == "sync" else dG
        res = []
        for (k0, kn) in khalves:
            si = wslot_state["i"] % 3
            wslot_state["i"] += 1
            if wslot_free[si] is not None:
                s, v = wslot_free[si]
                P.wait(eng, s, v)
            src = wblocks[m, :, k0 * mwidth:(k0 + kn) * mwidth]
            dstap = wbufs[si][:, 0:kn * mwidth]
            tgt = P.op(eng,
                       lambda e, d=dstap, s=src: e.dma_start(out=d, in_=s),
                       inc=sem, dma=True)
            res.append((si, k0, kn, tgt, sem))
        return res

    def big_matmul(wblocks, KC, MC, rhs_fn, evac_fn, mwidths=None,
                   eng="gpsimd"):
        banks = [(p0, "p0"), (p1, "p1"), (psB, "psB"), (psC, "psC")]
        bi = 0
        khalves = [(0, KC)] if KC <= 17 else [(0, 16), (16, KC - 16)]
        for m in range(MC):
            mw = 128 if mwidths is None else mwidths[m]
            blocks = stream_weight_block(wblocks, m, mw, khalves, eng=eng)
            for nh in range(2):
                bank, bname = banks[bi % len(banks)]
                bi += 1
                bank_wait(bname)
                first = True
                tgt = None
                for (si, k0, kn, dtgt, dsem) in blocks:
                    P.wait("tensor", dsem, dtgt)
                    for k in range(kn):
                        lhs = wbufs[si][:, k * mw:(k + 1) * mw]
                        rhs = rhs_fn(k0 + k, nh)
                        last = (k0 + k == KC - 1)
                        tgt = mm(bank[0:mw, 0:512], lhs, rhs,
                                 start=first, stop=last,
                                 inc=T if last else None)
                        first = False
                ev = evac_fn(m, nh, bank[0:mw, 0:512], tgt)
                bank_consumed(bname, *ev)
            for (si, k0, kn, dtgt, dsem) in blocks:
                wslot_free[si] = (T, T.n)

    psT_guard = [None]

    def psT_wait(sl=None):
        if psT_guard[0] is not None:
            s, v = psT_guard[0]
            P.wait("tensor", s, v)

    def psT_done(sem, val):
        psT_guard[0] = (sem, val)

    def emit_xload(tile):
        seq, t0 = tile
        src = x_in[seq, t0:t0 + TILE_T, :].rearrange("(g p) f -> p g f", p=128)
        dst = x_tok[:, :].rearrange("p (g f) -> p g f", g=8)
        return P.op("sync", lambda e, d=dst, s=src: e.dma_start(out=d, in_=s),
                    inc=dX, dma=True)

    def emit_ln(tile, ldx):
        import os
        lnpart = int(os.environ.get("LNPART", "9"))
        seq, t0 = tile
        if lnpart < 2:
            return
        P.wait("scalar", dX, ldx)
        P.wait("vector", dX, ldx)
        P.op("vector", lambda e: e.memset(sqa[:, :], 0.0), inc=V)
        for g in range(8):
            gc = slice(g * 1024, (g + 1) * 1024)
            P.op("vector",
                 lambda e, gc=gc, g=g: e.reduce_sum(
                     out=msum[:, g:g + 1], in_=x_tok[:, gc], axis=AXX), inc=V)
        P.drain("vector")
        P.op("vector", lambda e: e.tensor_scalar(
            out=mean_t[:, :], in0=msum[:, :], scalar1=-1.0 / 1024, scalar2=None,
            op0=ALU.mult), inc=V)
        P.drain("vector")
        P.wait("scalar", V)
        for g in range(8):
            gc = slice(g * 1024, (g + 1) * 1024)
            P.op("scalar",
                 lambda e, gc=gc, g=g: e.activation(
                     out=scrf[:, 0:1024], in_=x_tok[:, gc], func=AF.Square,
                     bias=mean_t[:, g:g + 1],
                     accum_out=sqa[:, g:g + 1]), inc=S)
        P.op("scalar", lambda e: e.activation(
            out=rstd_t[:, :], in_=sqa[:, :], func=AF.Sqrt, scale=1.0 / 1024,
            bias=cbias[:, 0:1]), inc=S)
        P.wait("vector", S)
        P.op("vector", lambda e: e.reciprocal(
            out=rstd_t[:, :], in_=rstd_t[:, :]), inc=V)
        P.drain("vector")
        if lnpart < 3:
            return
        for g in range(8):
            gc = slice(g * 1024, (g + 1) * 1024)
            P.op("vector",
                 lambda e, gc=gc, g=g: e.tensor_scalar(
                     out=xn_tok16[:, gc], in0=x_tok[:, gc],
                     scalar1=mean_t[:, g:g + 1], scalar2=rstd_t[:, g:g + 1],
                     op0=ALU.add, op1=ALU.mult), inc=V)
        if dbg:
            for i, tt in enumerate((sqa, msum, mean_t, m2t, vpre, rstd_t)):
                P.op("vector", lambda e, i=i, tt=tt: e.tensor_copy(
                    out=scrf[:, 1280 + i * 8:1280 + i * 8 + 8], in_=tt[:, :]),
                    inc=V)
            vst = V.n
            P.wait("sync", V, vst)
            P.op("sync", lambda e: e.dma_start(
                out=dbg["d_stats"][:, 0:48], in_=scrf[:, 1280:1328]),
                inc=dX, dma=True)
        vdone = V.n
        if lnpart < 4:
            return
        P.wait("tensor", V, vdone)
        for g in range(8):
            for c in range(8):
                sl = (g * 8 + c) % 8
                psT_wait(sl)
                tt = transpose(psT[:, sl * 128:(sl + 1) * 128],
                               xn_tok16[:, g * 1024 + c * 128:
                                        g * 1024 + (c + 1) * 128],
                               ident16[:, :], inc=T)
                P.wait("scalar", T, tt)
                P.op("scalar",
                     lambda e, g=g, c=c, sl=sl: e.copy(
                         out=xn16[:, c * 1024 + g * 128:c * 1024 + g * 128 + 128],
                         in_=psT[:, sl * 128:(sl + 1) * 128]), inc=S)
                psT_done(S, S.n)

    def emit_win_conv(tile):
        seq, t0 = tile
        barrier()

        def rhs(k, nh):
            return xn16[:, k * 1024 + nh * 512:k * 1024 + (nh + 1) * 512]

        def evac(m, nh, psum_ap, mmtgt):
            if m < 16:
                P.wait("scalar", T, mmtgt)
                P.op("scalar",
                     lambda e, m=m, nh=nh, p=psum_ap: e.activation(
                         out=siluz16[:, m * 1024 + nh * 512:
                                     m * 1024 + (nh + 1) * 512],
                         in_=p, func=AF.Silu), inc=S)
                return (S, S.n)
            if m < 33:
                c = m - 16
                cs = cstage[:, c % 2, :]
                P.wait("vector", T, mmtgt)
                P.op("vector",
                     lambda e, cs=cs, nh=nh, p=psum_ap: e.tensor_copy(
                         out=cs[:, 3 + nh * 512:3 + (nh + 1) * 512], in_=p),
                     inc=V)
                if dbg and c < 16:
                    P.op("vector",
                         lambda e, c=c, nh=nh, p=psum_ap: e.tensor_copy(
                             out=y16[:, c * 1024 + nh * 512:
                                     c * 1024 + (nh + 1) * 512], in_=p), inc=V)
                ev = (V, V.n)
                if nh == 1:
                    P.drain("vector")
                    P.op("vector",
                         lambda e, cs=cs, c=c, seq=seq: e.tensor_copy(
                             out=cs[:, 0:3], in_=tails[:, seq, c, :]), inc=V)
                    P.op("vector",
                         lambda e, cs=cs, c=c, seq=seq: e.tensor_copy(
                             out=tails[:, seq, c, :], in_=cs[:, 1024:1027]),
                         inc=V)
                    P.drain("vector")
                    P.wait("vector", S)
                    P.op("vector",
                         lambda e, cs=cs, c=c: e.tensor_scalar(
                             out=scrf[:, 0:1024], in0=cs[:, 0:1024],
                             scalar1=cw_s[:, c, 0:1], scalar2=None,
                             op0=ALU.mult), inc=V)
                    for k in range(1, 4):
                        P.op("vector",
                             lambda e, cs=cs, c=c, k=k: e.scalar_tensor_tensor(
                                 out=scrf[:, 0:1024], in0=cs[:, k:k + 1024],
                                 scalar=cw_s[:, c, k:k + 1], in1=scrf[:, 0:1024],
                                 op0=ALU.mult, op1=ALU.add), inc=V)
                    P.wait("scalar", V)
                    P.op("scalar",
                         lambda e, c=c: e.activation(
                             out=xbc16[:, c * 1024:(c + 1) * 1024],
                             in_=scrf[:, 0:1024], func=AF.Silu,
                             bias=cb_s[:, c:c + 1]), inc=S)
                    ev = (S, S.n)
                return ev
            P.wait("scalar", T, mmtgt)
            P.op("scalar",
                 lambda e, nh=nh, p=psum_ap: e.activation(
                     out=e32[:, nh * 512:(nh + 1) * 512], in_=p,
                     func=AF.Exp, bias=dtb_s[:, :]), inc=S)
            return (S, S.n)

        big_matmul(w_in_b, 8, 33, rhs, evac)
        big_matmul(w_dt_b, 8, 1, lambda k, nh: rhs(k, nh),
                   lambda m, nh, pa, mt: evac(33, nh, pa, mt),
                   mwidths=[32])
        P.op("scalar", lambda e: e.activation(
            out=dt32[:, :], in_=e32[:, :], func=AF.Ln, bias=cbias[0:32, 1:2]), inc=S)
        sdt = S.n
        P.wait("vector", S, sdt)
        P.op("vector", lambda e: e.tensor_scalar(
            out=dtA32[:, :], in0=dt32[:, :], scalar1=expA[:, :], scalar2=-1.0,
            op0=ALU.mult, op1=ALU.mult), inc=V)
        P.op("vector", lambda e: e.tensor_copy(out=dt16[:, :], in_=dt32[:, :]),
             inc=V)
        P.op("vector", lambda e: e.tensor_copy(
            out=cc16[:, :], in_=xbc16[64:128, 16 * 1024:17 * 1024]), inc=V)

    def emit_ssd(tile):
        seq, t0 = tile
        if t0 == 0:
            P.op("vector", lambda e: e.memset(hn32[:, :], 0.0), inc=V)
        barrier()
        prev_dh = [None, None]
        parityT = [None, None]
        parityS = [None, None]
        parityG = [None, None]
        prev_hnup = None
        prev_ymm = None
        prev_gp = None
        for q in range(8):
            p = q % 2
            qc = slice(q * 128, (q + 1) * 128)
            if parityT[p] is not None:
                P.wait("vector", T, parityT[p])
                P.wait("vector", S, parityS[p])
                P.wait("vector", G, parityG[p])
                P.wait("scalar", T, parityT[p])
                P.wait("tensor", S, parityS[p])
                P.wait("tensor", G, parityG[p])
            P.op("vector", lambda e, qc=qc, p=p: e.tensor_tensor_scan(
                out=c32[:, p, :], data0=dtA32[:, qc], data1=dtA32[:, qc],
                initial=0.0, op0=ALU.add, op1=ALU.bypass), inc=V)
            P.op("vector", lambda e, qc=qc, p=p: e.reduce_sum(
                out=clast32[:, p, :], in_=dtA32[:, qc], axis=AXX), inc=V)
            vc = V.n
            bank_wait("psA")
            P.wait("tensor", V, vc)
            transpose(psA[0:128, 128:160], c32[:, p, :], identf[0:32, 0:32],
                      inc=T)
            tc = transpose(psA[0:1, 192:224], clast32[:, p, :],
                           identf[0:32, 0:32], inc=T)
            P.wait("vector", T, tc)
            P.op("vector", lambda e, p=p: e.tensor_copy(
                out=ctok32[:, p, :], in_=psA[0:128, 128:160]), inc=V)
            P.op("vector", lambda e, p=p: e.tensor_copy(
                out=clrow32[:, p, :], in_=psA[0:1, 192:224]), inc=V)
            P.drain("vector")
            vc2 = V.n
            P.wait("tensor", V, vc2)
            tclb = mm(psA[0:128, 224:256], ones1[:, :], clrow32[:, p, :],
                      start=True, stop=True, inc=T)
            P.wait("vector", T, tclb)
            P.op("vector", lambda e, p=p: e.tensor_tensor(
                out=wpre[:, p, :], in0=psA[0:128, 224:256],
                in1=ctok32[:, p, :], op=ALU.subtract), inc=V)
            P.wait("scalar", V)
            P.op("scalar", lambda e, p=p: e.activation(
                out=wptok[:, p, :], in_=wpre[:, p, :], func=AF.Exp), inc=S)
            P.op("scalar", lambda e, p=p: e.activation(
                out=eqb32[:, p, :], in_=psA[0:64, 224:256], func=AF.Exp),
                inc=S)
            P.op("scalar", lambda e, p=p: e.activation(
                out=expc16[:, p, :], in_=c32[:, p, :], func=AF.Exp), inc=S)
            shead = S.n
            bank_consumed("psA", S, shead)
            bank_wait("psB")
            smm = mm(psB[0:128, 128:256],
                     xbc16[0:64, 16 * 1024 + q * 128:16 * 1024 + (q + 1) * 128],
                     cc16[:, qc], start=True, stop=True, inc=T)
            P.wait("vector", T, smm)
            P.op("vector", lambda e, p=p: e.tensor_tensor(
                out=sm16b[:, p, :], in0=psB[0:128, 128:256], in1=u01[:, :],
                op=ALU.mult), inc=V)
            sm_done = V.n
            bank_consumed("psB", V, sm_done)
            psT_wait(4)
            tb = transpose(psT[:, 512:576],
                           xbc16[0:64, 16 * 1024 + q * 128:
                                 16 * 1024 + (q + 1) * 128],
                           ident16[0:64, 0:64], inc=T)
            P.wait("scalar", T, tb)
            P.op("scalar", lambda e, p=p: e.copy(out=btok16[:, p, :],
                                                 in_=psT[:, 512:576]), inc=S)
            btok_done = S.n
            psT_done(S, btok_done)
            pb = [(psB, "psB"), (psC, "psC")]
            for c in range(16):
                b, bn = pb[c % 2]
                bank_wait(bn)
                mm(b[0:64, 0:128],
                   i32b[:, 2 * c:2 * c + 1].broadcast_to([32, 64]),
                   dt16[:, qc], start=True, stop=True)
                m2 = mm(b[64:128, 0:128],
                        i32b[:, 2 * c + 1:2 * c + 2].broadcast_to([32, 64]),
                        dt16[:, qc], start=True, stop=True, inc=T)
                P.wait("vector", T, m2)
                P.op("vector",
                     lambda e, c=c, b=b, q=q: e.tensor_tensor(
                         out=xds[:, c % 2, :],
                         in0=xbc16[:, c * 1024 + q * 128:
                                   c * 1024 + (q + 1) * 128],
                         in1=b[0:128, 0:128], op=ALU.mult), inc=V)
                bank_consumed(bn, V, V.n)
                P.wait("tensor", V)
                sl = c % 4
                psT_wait(sl)
                tt = transpose(psT[:, sl * 128:(sl + 1) * 128],
                               xds[:, c % 2, :], ident16[:, :], inc=T)
                P.wait("scalar", T, tt)
                P.op("scalar",
                     lambda e, c=c, sl=sl: e.copy(
                         out=xd16[:, c * 128:(c + 1) * 128],
                         in_=psT[:, sl * 128:(sl + 1) * 128]), inc=S)
                psT_done(S, S.n)
            hb = [(psD, "psD"), (psE, "psE")]
            dbk = [(psB, "psB"), (psC, "psC")]
            for h in range(32):
                X, Xn = hb[h % 2]
                DB, DBn = dbk[h % 2]
                hcol = slice(h * 64, (h + 1) * 64)
                xs_half = slice((h % 2) * 64, (h % 2) * 64 + 64)
                xchunk = h // 2
                bank_wait(DBn)
                bcd = mm(DB[0:128, 0:128],
                         i32f[:, h:h + 1].broadcast_to([32, 128]),
                         c32[:, p, :], start=True, stop=True, inc=T)
                P.wait("vector", T, bcd)
                P.op("vector",
                     lambda e, DB=DB, h=h, p=p: e.tensor_scalar(
                         out=dm32[:, :], in0=DB[0:128, 0:128],
                         scalar1=ctok32[:, p, h:h + 1], scalar2=0.0,
                         op0=ALU.subtract, op1=ALU.min), inc=V)
                bank_consumed(DBn, V, V.n)
                P.wait("scalar", V)
                P.op("scalar", lambda e: e.activation(
                    out=m16t[:, :], in_=dm32[:, :], func=AF.Exp), inc=S)
                P.wait("vector", S)
                if prev_ymm is not None:
                    P.wait("vector", T, prev_ymm)
                P.op("vector", lambda e, p=p: e.tensor_tensor(
                    out=g16[:, :], in0=m16t[:, :], in1=sm16b[:, p, :],
                    op=ALU.mult), inc=V)
                g_done = V.n
                if prev_hnup is not None and h < 2:
                    P.wait("scalar", V, prev_hnup)
                P.op("scalar",
                     lambda e, h=h, hcol=hcol: e.copy(
                         out=hn16tmp[:, h % 2, :], in_=hn32[:, hcol]), inc=S)
                hcopy_done = S.n
                bank_wait(Xn)
                ce_mm = mm(X[0:64, 384:512],
                           i32b[:, h:h + 1].broadcast_to([32, 64]),
                           expc16[:, p, :], start=True, stop=True, inc=T)
                P.wait("vector", T, ce_mm)
                P.op("vector",
                     lambda e, X=X, qc=qc, h=h: e.tensor_tensor(
                         out=ce16[:, h % 2, :], in0=cc16[:, qc],
                         in1=X[0:64, 384:512], op=ALU.mult), inc=V)
                ce_done = V.n
                P.wait("scalar", S, btok_done)
                if prev_dh[h % 2] is not None:
                    P.wait("scalar", T, prev_dh[h % 2])
                P.op("scalar",
                     lambda e, h=h, p=p: e.activation(
                         out=bt16[:, h % 2, :], in_=btok16[:, p, :],
                         func=AF.Copy, scale=wptok[:, p, h:h + 1]), inc=S)
                bt_done = S.n
                P.wait("tensor", V, g_done)
                P.wait("tensor", V, ce_done)
                P.wait("tensor", S, hcopy_done)
                mm(X[0:64, 0:128], xd16[:, hcol], g16[:, :],
                   start=True, stop=False)
                ymm = mm(X[0:64, 0:128], hn16tmp[:, h % 2, :],
                         ce16[:, h % 2, :], start=False, stop=True, inc=T)
                prev_ymm = ymm
                P.wait("tensor", S, bt_done)
                dhmm = mm(X[0:64, 128:192], bt16[:, h % 2, :], xd16[:, hcol],
                          start=True, stop=True, inc=T)
                prev_dh[h % 2] = dhmm
                P.wait("vector", T, dhmm)
                P.op("vector",
                     lambda e, X=X, h=h, hcol=hcol, p=p: e.scalar_tensor_tensor(
                         out=hn32[:, hcol], in0=hn32[:, hcol],
                         scalar=eqb32[:, p, h:h + 1], in1=X[0:64, 128:192],
                         op0=ALU.mult, op1=ALU.add), inc=V)
                prev_hnup = V.n
                P.op("vector",
                     lambda e, X=X, h=h, q=q, xs_half=xs_half, xchunk=xchunk:
                     e.scalar_tensor_tensor(
                         out=y16[xs_half, xchunk * 1024 + q * 128:
                                 xchunk * 1024 + (q + 1) * 128],
                         in0=xbc16[xs_half, xchunk * 1024 + q * 128:
                                   xchunk * 1024 + (q + 1) * 128],
                         scalar=dp_s[xs_half, h:h + 1], in1=X[0:64, 0:128],
                         op0=ALU.mult, op1=ALU.add), inc=V)
                bank_consumed(Xn, V, V.n)
            parityT[p] = T.n
            parityS[p] = S.n
            parityG[p] = G.n

    def emit_gate_rms_wout(tile):
        barrier()
        for c in range(16):
            cc = slice(c * 1024, (c + 1) * 1024)
            P.op("vector",
                 lambda e, cc=cc: e.tensor_tensor(
                     out=y16[:, cc], in0=y16[:, cc], in1=siluz16[:, cc],
                     op=ALU.mult), inc=V)
        yg_done = V.n
        P.wait("scalar", V, yg_done)
        for c in range(16):
            cc = slice(c * 1024, (c + 1) * 1024)
            P.op("scalar",
                 lambda e, cc=cc: e.activation(
                     out=siluz16[:, cc], in_=y16[:, cc], func=AF.Square),
                 inc=S)
        sq_done = S.n
        for nh in range(2):
            bank, bname = [(p0, "p0"), (p1, "p1")][nh]
            bank_wait(bname)
            P.wait("tensor", S, sq_done)
            tgt = None
            for c in range(16):
                tgt = mm(bank[0:128, 0:512], ones16[:, :],
                         siluz16[:, c * 1024 + nh * 512:
                                 c * 1024 + (nh + 1) * 512],
                         start=(c == 0), stop=(c == 15),
                         inc=T if c == 15 else None)
            P.wait("scalar", T, tgt)
            P.op("scalar",
                 lambda e, bank=bank, nh=nh: e.activation(
                     out=scrf[:, nh * 512:(nh + 1) * 512],
                     in_=bank[0:128, 0:512], func=AF.Sqrt,
                     bias=cbias[:, 0:1], scale=1.0 / 2048), inc=S)
            bank_consumed(bname, S, S.n)
        rstd_done = S.n
        P.wait("vector", S, rstd_done)
        P.op("vector", lambda e: e.reciprocal(
            out=scrf[:, 0:1024], in_=scrf[:, 0:1024]), inc=V)
        for c in range(16):
            cc = slice(c * 1024, (c + 1) * 1024)
            P.op("vector",
                 lambda e, cc=cc, c=c: e.scalar_tensor_tensor(
                     out=y16[:, cc], in0=y16[:, cc], scalar=nw_s[:, c:c + 1],
                     in1=scrf[:, 0:1024], op0=ALU.mult, op1=ALU.mult), inc=V)
        yn_done = V.n
        P.wait("tensor", V, yn_done)

        def rhs(k, nh):
            return y16[:, k * 1024 + nh * 512:k * 1024 + (nh + 1) * 512]

        def evac(m, nh, psum_ap, mmtgt):
            P.wait("vector", T, mmtgt)
            sl = slice(m * 1024 + nh * 512, m * 1024 + (nh + 1) * 512)
            P.op("vector",
                 lambda e, sl=sl, p=psum_ap: e.tensor_tensor(
                     out=hm16[:, sl], in0=p, in1=xn16[:, sl], op=ALU.add),
                 inc=V)
            return (V, V.n)

        big_matmul(w_out_b, 16, 8, rhs, evac)

    def emit_mlp(tile):
        barrier()
        for c in range(8):
            cc = slice(c * 1024, (c + 1) * 1024)
            P.op("scalar",
                 lambda e, cc=cc: e.activation(
                     out=hn16[:, cc], in_=hm16[:, cc], func=AF.Square), inc=S)
        sq_done = S.n
        for nh in range(2):
            bank, bname = [(p0, "p0"), (p1, "p1")][nh]
            bank_wait(bname)
            P.wait("tensor", S, sq_done)
            tgt = None
            for c in range(8):
                tgt = mm(bank[0:128, 0:512], ones16[:, :],
                         hn16[:, c * 1024 + nh * 512:c * 1024 + (nh + 1) * 512],
                         start=(c == 0), stop=(c == 7),
                         inc=T if c == 7 else None)
            P.wait("scalar", T, tgt)
            P.op("scalar",
                 lambda e, bank=bank, nh=nh: e.activation(
                     out=hscr[:, nh * 512:(nh + 1) * 512],
                     in_=bank[0:128, 0:512], func=AF.Sqrt,
                     bias=cbias[:, 0:1], scale=1.0 / 1024), inc=S)
            bank_consumed(bname, S, S.n)
        rstd_done = S.n
        P.wait("vector", S, rstd_done)
        P.op("vector", lambda e: e.reciprocal(
            out=hscr[:, 0:1024], in_=hscr[:, 0:1024]), inc=V)
        for c in range(8):
            cc = slice(c * 1024, (c + 1) * 1024)
            P.op("vector",
                 lambda e, cc=cc, c=c: e.scalar_tensor_tensor(
                     out=hn16[:, cc], in0=hm16[:, cc], scalar=rw_s[:, c:c + 1],
                     in1=hscr[:, 0:1024], op0=ALU.mult, op1=ALU.mult), inc=V)
        hn_done = V.n
        P.wait("tensor", V, hn_done)

        def rhs1(k, nh):
            return hn16[:, k * 1024 + nh * 512:k * 1024 + (nh + 1) * 512]

        def evac1(m, nh, psum_ap, mmtgt):
            P.wait("scalar", T, mmtgt)
            P.op("scalar",
                 lambda e, m=m, nh=nh, p=psum_ap: e.activation(
                     out=m116[:, m * 1024 + nh * 512:m * 1024 + (nh + 1) * 512],
                     in_=p, func=AF.Silu, bias=b1_s[:, m:m + 1]), inc=S)
            return (S, S.n)

        big_matmul(mlp1_b, 8, 32, rhs1, evac1)
        barrier()

        def rhs2(k, nh):
            return m116[:, k * 1024 + nh * 512:k * 1024 + (nh + 1) * 512]

        def evac2(m, nh, psum_ap, mmtgt):
            P.wait("vector", T, mmtgt)
            sl = slice(m * 1024 + nh * 512, m * 1024 + (nh + 1) * 512)
            P.op("vector",
                 lambda e, sl=sl, p=psum_ap, m=m: e.scalar_tensor_tensor(
                     out=acc16[:, sl], in0=p, scalar=b2_s[:, m:m + 1],
                     in1=hm16[:, sl], op0=ALU.add, op1=ALU.add), inc=V)
            return (V, V.n)

        big_matmul(mlp2_b, 32, 8, rhs2, evac2)

    def emit_ffn(tile):
        barrier()
        banks = [(p0, "p0"), (p1, "p1"), (psB, "psB"), (psC, "psC")]
        bi = 0
        for m in range(32):
            bl1 = stream_weight_block(fw1_b, m, 128, [(0, 8)])
            bl2 = stream_weight_block(fw2_b, m, 128, [(0, 8)])
            for nh in range(2):
                ba, bna = banks[bi % 4]
                bb, bnb = banks[(bi + 1) % 4]
                bi += 2
                bank_wait(bna)
                t1 = None
                (si, k0, kn, dtgt, dsem) = bl1[0]
                P.wait("tensor", dsem, dtgt)
                for k in range(8):
                    t1 = mm(ba[0:128, 0:512], wbufs[si][:, k * 128:(k + 1) * 128],
                            xn16[:, k * 1024 + nh * 512:k * 1024 + (nh + 1) * 512],
                            start=(k == 0), stop=(k == 7),
                            inc=T if k == 7 else None)
                bank_wait(bnb)
                t2 = None
                (si2, k0, kn, dtgt2, dsem2) = bl2[0]
                P.wait("tensor", dsem2, dtgt2)
                for k in range(8):
                    t2 = mm(bb[0:128, 0:512], wbufs[si2][:, k * 128:(k + 1) * 128],
                            xn16[:, k * 1024 + nh * 512:k * 1024 + (nh + 1) * 512],
                            start=(k == 0), stop=(k == 7),
                            inc=T if k == 7 else None)
                P.wait("scalar", T, t1)
                P.wait("scalar", V)
                P.op("scalar",
                     lambda e, m=m, nh=nh, ba=ba: e.activation(
                         out=s1f[:, nh * 512:(nh + 1) * 512],
                         in_=ba[0:128, 0:512], func=AF.Silu,
                         scale=s1_s[:, m:m + 1]), inc=S)
                s1done = S.n
                bank_consumed(bna, S, s1done)
                P.wait("vector", T, t2)
                P.wait("vector", S, s1done)
                P.op("vector",
                     lambda e, m=m, nh=nh, bb=bb: e.scalar_tensor_tensor(
                         out=h16[:, m * 1024 + nh * 512:m * 1024 + (nh + 1) * 512],
                         in0=bb[0:128, 0:512], scalar=s2_s[:, m:m + 1],
                         in1=s1f[:, nh * 512:(nh + 1) * 512],
                         op0=ALU.mult, op1=ALU.mult), inc=V)
                bank_consumed(bnb, V, V.n)
            for (si, k0, kn, dtgt, dsem) in bl1 + bl2:
                wslot_free[si] = (T, T.n)

        barrier()

        def rhs_h(k, nh):
            return h16[:, k * 1024 + nh * 512:k * 1024 + (nh + 1) * 512]

        def evac_g1(m, nh, psum_ap, mmtgt):
            P.wait("scalar", T, mmtgt)
            P.op("scalar",
                 lambda e, m=m, nh=nh, p=psum_ap: e.activation(
                     out=g116[:, m * 1024 + nh * 512:m * 1024 + (nh + 1) * 512],
                     in_=p, func=AF.Gelu, scale=sh1_s[:, m:m + 1]), inc=S)
            return (S, S.n)

        big_matmul(fh1_b, 32, 32, rhs_h, evac_g1)
        barrier()

        def rhs_g1(k, nh):
            return g116[:, k * 1024 + nh * 512:k * 1024 + (nh + 1) * 512]

        def evac_g2(m, nh, psum_ap, mmtgt):
            P.wait("scalar", T, mmtgt)
            P.op("scalar",
                 lambda e, m=m, nh=nh, p=psum_ap: e.activation(
                     out=h16[:, m * 1024 + nh * 512:m * 1024 + (nh + 1) * 512],
                     in_=p, func=AF.Gelu, scale=sh2_s[:, m:m + 1]), inc=S)
            return (S, S.n)

        big_matmul(fh2_b, 32, 32, rhs_g1, evac_g2)
        barrier()

        def evac_w3(m, nh, psum_ap, mmtgt):
            P.wait("vector", T, mmtgt)
            sl = slice(m * 1024 + nh * 512, m * 1024 + (nh + 1) * 512)
            P.op("vector",
                 lambda e, sl=sl, p=psum_ap, m=m: e.scalar_tensor_tensor(
                     out=acc16[:, sl], in0=p, scalar=s3_s[:, m:m + 1],
                     in1=acc16[:, sl], op0=ALU.mult, op1=ALU.add), inc=V)
            return (V, V.n)

        big_matmul(fw3_b, 32, 8, rhs_h, evac_w3)

    def emit_final(tile):
        seq, t0 = tile
        barrier()
        src = x_in[seq, t0:t0 + TILE_T, :].rearrange("(g p) f -> p g f", p=128)
        dst = x_re[:, :].rearrange("p (g f) -> p g f", g=8)
        ldx = P.op("sync", lambda e, d=dst, s=src: e.dma_start(out=d, in_=s),
                   inc=dX, dma=True)
        P.wait("vector", dX, ldx)
        for c in range(8):
            for g in range(8):
                sl = (c * 8 + g) % 8
                psT_wait(sl)
                tt = transpose(psT[:, sl * 128:(sl + 1) * 128],
                               acc16[:, c * 1024 + g * 128:
                                     c * 1024 + g * 128 + 128],
                               ident16[:, :], inc=T)
                P.wait("vector", T, tt)
                P.op("vector",
                     lambda e, c=c, g=g, sl=sl: e.tensor_tensor(
                         out=x_re[:, g * 1024 + c * 128:g * 1024 + c * 128 + 128],
                         in0=x_re[:, g * 1024 + c * 128:g * 1024 + c * 128 + 128],
                         in1=psT[:, sl * 128:(sl + 1) * 128], op=ALU.add),
                     inc=V)
                psT_done(V, V.n)
        vout = V.n
        P.wait("sync", V, vout)
        osrc = x_re[:, :].rearrange("p (g f) -> p g f", g=8)
        odst = out_t[seq, t0:t0 + TILE_T, :].rearrange("(g p) f -> p g f", p=128)
        P.op("sync", lambda e, d=odst, s=osrc: e.dma_start(out=d, in_=s),
             inc=dX, dma=True)

    def dbg_dump(name, buf):
        if not dbg:
            return
        P.wait("sync", V)
        P.wait("sync", S)
        P.wait("sync", T)
        P.op("sync", lambda e, d=dbg[name], s=buf: e.dma_start(out=d, in_=s[:, :]),
             inc=dX, dma=True)

    load_consts()
    tiles = [(0, 0), (0, 1024), (1, 0), (1, 1024)][:ntiles]
    ldx0 = emit_xload(tiles[0]) if upto >= 1 else None
    for ti, tile in enumerate(tiles):
        if upto >= 1:
            ldx = ldx0 if ti == 0 else emit_xload(tile)
            emit_ln(tile, ldx)
            if debug and ti == 0:
                dbg_dump("d_xn16", xn16)
        if upto >= 2:
            emit_win_conv(tile)
            if debug and ti == 0:
                dbg_dump("d_siluz", siluz16)
                dbg_dump("d_xbc", xbc16)
                dbg_dump("d_dt", dt32)
                if upto == 2:
                    dbg_dump("d_y", y16)
        if upto >= 3:
            emit_ssd(tile)
            if debug and ti == 0:
                dbg_dump("d_y", y16)
        if upto >= 4:
            emit_gate_rms_wout(tile)
            if debug and ti == 0:
                dbg_dump("d_yn", y16)
                dbg_dump("d_hm", hm16)
        if upto >= 5:
            emit_mlp(tile)
            if debug and ti == 0:
                dbg_dump("d_m1", m116)
        if upto >= 6:
            emit_ffn(tile)
            if debug and ti == 0:
                dbg_dump("d_h16", h16)
                dbg_dump("d_g1", g116)
                dbg_dump("d_acc", acc16)
        if upto >= 7:
            emit_final(tile)
        barrier()
    P.wait("sync", dX)
    P.wait("sync", dW)
    P.wait("sync", dG)

    with nc.Block() as block:
        @block.sync
        def _(e):
            P.run("sync", e)

        @block.tensor
        def _(e):
            P.run("tensor", e)

        @block.vector
        def _(e):
            P.run("vector", e)

        @block.scalar
        def _(e):
            P.run("scalar", e)

        @block.gpsimd
        def _(e):
            P.run("gpsimd", e)

    for g in guards:
        g.__exit__(None, None, None)
    return nc


def _blockify(W, KC, mw=128):
    I, O = W.shape
    MC = O // mw
    return np.ascontiguousarray(
        W.reshape(KC, 128, MC, mw).transpose(2, 1, 0, 3).reshape(
            MC, 128, KC * mw).astype(np.float32))


_SHARED = {}


def _shared_inputs(inputs):
    f32 = np.float32
    g = lambda k: np.ascontiguousarray(np.asarray(inputs[k], dtype=f32))
    d = {}
    w_in = g("L0_attn_W_in")
    d["w_in_b"] = _blockify(w_in[:, :4224], 8)
    d["w_dt_b"] = _blockify(w_in[:, 4224:], 8, mw=32)
    d["w_out_b"] = _blockify(g("L0_attn_W_out"), 16)
    d["mlp1_b"] = _blockify(g("L0_attn_mlp_w1"), 8)
    d["mlp2_b"] = _blockify(g("L0_attn_mlp_w2"), 32)
    d["fw1_b"] = _blockify(g("L0_ffn_w1_w").T, 8)
    d["fw2_b"] = _blockify(g("L0_ffn_w2_w").T, 8)
    d["fh1_b"] = _blockify(g("L0_ffn_h1_w").T, 32)
    d["fh2_b"] = _blockify(g("L0_ffn_h2_w").T, 32)
    d["fw3_b"] = _blockify(g("L0_ffn_w3_w").T, 32)
    d["conv_w"] = np.ascontiguousarray(g("L0_attn_conv_w").reshape(17, 128, 4))
    d["conv_b"] = np.ascontiguousarray(g("L0_attn_conv_b").reshape(17, 128))
    d["dt_bias"] = np.ascontiguousarray(g("L0_attn_dt_bias").reshape(32, 1))
    d["A_log"] = np.ascontiguousarray(g("L0_attn_A_log").reshape(32, 1))
    d["D_param"] = np.ascontiguousarray(
        np.broadcast_to(g("L0_attn_D_param").reshape(1, 32), (128, 32)))
    d["norm_w"] = np.ascontiguousarray(g("L0_attn_norm_w").reshape(16, 128))
    d["rms_w"] = np.ascontiguousarray(g("L0_attn_rms_w").reshape(8, 128))
    d["mlp_b1"] = np.ascontiguousarray(g("L0_attn_mlp_b1").reshape(32, 128))
    d["mlp_b2"] = np.ascontiguousarray(g("L0_attn_mlp_b2").reshape(8, 128))
    d["f_s1"] = np.ascontiguousarray(g("L0_ffn_w1_s").reshape(32, 128))
    d["f_s2"] = np.ascontiguousarray(g("L0_ffn_w2_s").reshape(32, 128))
    d["f_sh1"] = np.ascontiguousarray(g("L0_ffn_h1_s").reshape(32, 128))
    d["f_sh2"] = np.ascontiguousarray(g("L0_ffn_h2_s").reshape(32, 128))
    d["f_s3"] = np.ascontiguousarray(g("L0_ffn_w3_s").reshape(8, 128))
    return d


def _host_inputs(inputs, core):
    if "d" not in _SHARED:
        _SHARED["d"] = _shared_inputs(inputs)
    d = dict(_SHARED["d"])
    x = np.ascontiguousarray(np.asarray(inputs["L0_attn_hidden_tokens"],
                                        dtype=np.float32))
    d["x"] = np.ascontiguousarray(x[2 * core:2 * core + 2])
    return d


def kernel(**inputs):
    _SHARED.clear()
    nc = _build(debug=False)
    in_maps = [_host_inputs(inputs, i) for i in range(NCORES)]
    res = run_bass_kernel_spmd(nc, in_maps, core_ids=list(range(NCORES)))
    out = np.concatenate([np.asarray(r["out"]) for r in res.results], axis=0)
    return np.ascontiguousarray(out.astype(np.float32))
